# version 10
# baseline (speedup 1.0000x reference)
import numpy as np
import ml_dtypes
from contextlib import ExitStack

import concourse.bass as bass
import concourse.mybir as mybir
from concourse.bass_utils import run_bass_kernel_spmd

F32 = mybir.dt.float32
BF16 = mybir.dt.bfloat16
AF = mybir.ActivationFunctionType
ALU = mybir.AluOpType
AX = mybir.AxisListType

S = 8192
D = 2048
HV, HK, DK, DV = 32, 16, 128, 128
KEY = HK * DK
VAL = HV * DV
KS = 4
EPS = 1e-6
LN_EPS = 1e-5
C = 128
TILE = 512
NT = S // TILE
CPT = TILE // C
NCORE = 8
MASKVAL = -1e30
LNCLAMP = float(-np.log(EPS))
S_SCALE = float(-0.5 * np.log(DK))
DCLAMP = 1e30

ENGINES = ("sync", "gpsimd", "tensor", "vector", "scalar")


class Sched:
    def __init__(self):
        self.streams = {e: [] for e in ENGINES}

    def on(self, eng):
        def reg(fn):
            self.streams[eng].append(fn)
            return None
        return reg


class Ctr:
    def __init__(self, ctx, nc, name):
        self.h = ctx.enter_context(nc.semaphore(name))
        self.n = 0


_CHAIN_OPS = {
    "tensor_copy", "tensor_tensor", "tensor_add", "tensor_sub", "tensor_mul",
    "tensor_max", "tensor_scalar", "tensor_single_scalar", "tensor_scalar_add",
    "tensor_scalar_sub", "tensor_scalar_mul", "tensor_scalar_max",
    "tensor_scalar_min", "tensor_relu", "tensor_reduce", "reduce_max",
    "reduce_sum", "scalar_tensor_tensor", "activation", "memset", "memzero",
    "reciprocal", "select", "sign", "square", "sqrt", "mul", "copy", "add",
}


class InstProxy:
    def __init__(self, rec, chained, inst):
        self._rec = rec
        self._chained = chained
        self._inst = inst

    def then_inc(self, sem, k=1):
        if self._chained:
            self._rec._milestone(sem, k)
        else:
            self._inst.then_inc(sem, k)
        return self


class EngRec:
    def __init__(self, e, chain=None):
        self._e = e
        self._last = None
        self._last_ch = False
        self._chain = chain
        self._cn = 0

    def __getattr__(self, name):
        f = getattr(self._e, name)
        if callable(f):
            chain = self._chain is not None and name in _CHAIN_OPS

            def wrap(*a, **k):
                if chain and self._cn:
                    self._e.wait_ge(self._chain, self._cn)
                r = f(*a, **k)
                if chain:
                    r.then_inc(self._chain, 1)
                    self._cn += 1
                self._last = r
                self._last_ch = chain
                return InstProxy(self, chain, r)
            return wrap
        return f

    def _milestone(self, sem, k):
        self._e.wait_ge(self._chain, self._cn)
        self._e.sem_inc(sem, k)

    def fin(self, sem, k=1):
        if self._last_ch:
            self._milestone(sem, k)
        else:
            self._last.then_inc(sem, k)


def _bf(x):
    return np.ascontiguousarray(np.asarray(x).astype(ml_dtypes.bfloat16))


def _pretile(WT, nK, nM):
    t = WT.reshape(nK, 128, nM, 128).transpose(1, 2, 0, 3)
    return np.ascontiguousarray(t.reshape(128, nM * nK * 128), dtype=np.float32)



def build_fused(nt=NT):
    nc = bass.Bass("TRN2", target_bir_lowering=False)
    dt = nc.dram_tensor

    x_tok = dt("x_tok", [S, D], F32, kind="ExternalInput").ap()
    wqkv_p = dt("wqkv_p", [128, 8 * 16 * 128], F32, kind="ExternalInput").ap()
    wzab_p = dt("wzab_p", [128, 16 * 520], F32, kind="ExternalInput").ap()
    conv_in = dt("conv_in", [128, 32], F32, kind="ExternalInput").ap()
    alog_r = dt("alog_r", [1, 4], F32, kind="ExternalInput").ap()
    dtb_r = dt("dtb_r", [1, 4], F32, kind="ExternalInput").ap()
    nw_in = dt("nw_in", [128, 512], F32, kind="ExternalInput").ap()
    mS_in = dt("mS_in", [128, 128], F32, kind="ExternalInput").ap()
    mI_in = dt("mI_in", [128, 128], F32, kind="ExternalInput").ap()
    I_in = dt("I_in", [128, 128], F32, kind="ExternalInput").ap()
    U_in = dt("U_in", [128, 128], F32, kind="ExternalInput").ap()
    or_in = dt("or_in", [33, 128], F32, kind="ExternalInput").ap()
    oc_in = dt("oc_in", [128, 1], F32, kind="ExternalInput").ap()
    L_in = dt("L_in", [128, 128], F32, kind="ExternalInput").ap()

    x2 = dt("x2", [TOK2, D], F32, kind="ExternalInput").ap()
    xT2 = dt("xT2", [D, TOK2], F32, kind="ExternalInput").ap()
    w1_p = dt("w1_p", [128, 32 * 16 * 128], F32, kind="ExternalInput").ap()
    w2_p = dt("w2_p", [128, 32 * 32 * 128], F32, kind="ExternalInput").ap()
    w3_p = dt("w3_p", [128, 16 * 32 * 128], F32, kind="ExternalInput").ap()
    wo_p = dt("wo_p", [128, 16 * 32 * 128], F32, kind="ExternalInput").ap()
    b1_in = dt("b1_in", [128, 32], F32, kind="ExternalInput").ap()
    b2_in = dt("b2_in", [128, 32], F32, kind="ExternalInput").ap()
    b3_in = dt("b3_in", [128, 16], F32, kind="ExternalInput").ap()

    outf = dt("outf", [D, TOK2], F32, kind="ExternalOutput").ap()

    cc_in_h = dt("cc_in", [S, 512], BF16)
    cc_out_h = dt("cc_out", [S, 512], BF16)
    og = cc_in_h.ap()
    cc_out = cc_out_h.ap()

    xn_d = dt("xn_d", [S, D], BF16).ap()
    kv_d = dt("kv_d", [2, 768, TILE], BF16).ap()
    row_d = dt("row_d", [2, 2, 2048], F32).ap()
    xn2_d = dt("xn2_d", [TOK2, D], BF16).ap()

    sems = ExitStack()
    ctx = ExitStack()
    sb = lambda n_, sh, d_=F32: ctx.enter_context(nc.sbuf_tensor(n_, sh, d_))
    psa = lambda n_, sh, d_=F32: ctx.enter_context(nc.psum_tensor(n_, sh, d_))

    wqkv = sb("wqkv", [128, 8 * 16 * 128], BF16)
    wzab = sb("wzab", [128, 16 * 520], BF16)
    conv_w = sb("conv_w", [128, 32])
    xt = [sb(f"xt{i}", [128, D]) for i in range(2)]
    xnt = [sb(f"xnt{i}", [128, D], BF16) for i in range(2)]
    lnst = sb("lnst", [128, 8])
    xnf = sb("xnf", [128, 16 * TILE], BF16)
    qkvr = sb("qkvr", [128, 8 * (TILE + 4)], BF16)
    halo = sb("halo", [128, 8 * 4], BF16)
    convt = sb("convt", [128, TILE])
    mixed = [sb(f"mixed{i}", [128, 8 * TILE], BF16) for i in range(2)]
    ksq = sb("ksq", [128, TILE])
    vtok = [sb(f"vtok{i}", [128, CPT * 512], BF16) for i in range(2)]
    ktok = [sb(f"ktok{i}", [128, CPT * 256], BF16) for i in range(2)]
    zt = [sb(f"zt{i}", [128, CPT * 512], BF16) for i in range(2)]
    ogt = sb("ogt", [128, 512], BF16)

    gpl = sb("gpl", [128, 16])
    bpl = sb("bpl", [128, 16])
    cgc = sb("cgc", [128, 16])
    rvc = sb("rvc", [128, 16])
    cpl = sb("cpl", [128, 16])
    gam = sb("gam", [128, 16])
    lrkp = sb("lrkp", [128, 8])
    rkp = sb("rkp", [128, 8])
    lrqp = sb("lrqp", [128, 8])
    lnssq = sb("lnssq", [128, 16])
    brk = sb("brk", [128, 16])
    gban = sb("gban", [128, 16])
    gqa = sb("gqa", [128, 16])
    pla = sb("pla", [128, 16])
    plw = sb("plw", [128, 16])
    pl16 = sb("pl16", [128, 16])
    sp4 = sb("sp4", [128, 4])
    ab_sb = sb("ab_sb", [128, 32])
    nae_row = sb("nae_row", [1, 4])
    dtb_row = sb("dtb_row", [1, 4])
    nae_b = sb("nae_b", [128, 4])
    dtb_b = sb("dtb_b", [128, 4])
    rowA = sb("rowA", [16, 128])
    rowW = sb("rowW", [16, 128])
    flat = sb("flat", [33, 4096])

    exa = sb("exa", [128, 512])
    exw = sb("exw", [128, 512])
    dmt = sb("dmt", [128, 512])
    dmt2 = sb("dmt2", [128, 512])
    o1 = sb("o1", [128, 512])
    osb = sb("osb", [128, 1024])
    osq = sb("osq", [128, 512])
    rms = sb("rms", [128, 8])
    at_bf = sb("at_bf", [128, 4 * 512], BF16)
    a_bf = sb("a_bf", [128, 4 * 512], BF16)
    s2_bf = sb("s2_bf", [128, 1024], BF16)
    s3_bf = sb("s3_bf", [128, 1024], BF16)
    b_bf = sb("b_bf", [128, 512], BF16)
    x_bf = sb("x_bf", [128, 512], BF16)
    wt_bf = sb("wt_bf", [128, 4 * 512], BF16)
    dc_bf = sb("dc_bf", [128, 512], BF16)
    kh_bf = sb("kh_bf", [128, 4 * 256], BF16)
    st_f = sb("st_f", [128, 512])
    st_b = sb("st_b", [128, 512], BF16)
    idb = sb("idb", [128, 128], BF16)
    mi_sb = sb("mi_sb", [128, 128])
    ms_sb = sb("ms_sb", [128, 128])
    i_sb = sb("i_sb", [128, 128])
    u_sb = sb("u_sb", [128, 128])
    l_sb = sb("l_sb", [128, 128])
    or_sb = sb("or_sb", [33, 128])
    oc_sb = sb("oc_sb", [128, 1])
    nw_sb = sb("nw_sb", [128, 512])
    zsil4 = sb("zsil4", [128, CPT * 512], BF16)

    PJ = psa("PJ", [128, 512])
    PZ = psa("PZ", [128, 512])
    P2 = psa("P2", [128, 512])
    B3 = psa("B3", [128, 512])
    B4 = psa("B4", [128, 512])
    B5 = psa("B5", [128, 512])
    B6 = psa("B6", [128, 512])
    XB = psa("XB", [128, 512], BF16)

    spre = Ctr(sems, nc, "spre")
    sx = [Ctr(sems, nc, "sx0"), Ctr(sems, nc, "sx1")]
    sxw = [Ctr(sems, nc, "sxw0"), Ctr(sems, nc, "sxw1")]
    sxr = Ctr(sems, nc, "sxr")
    skw = Ctr(sems, nc, "skw")
    skr = [Ctr(sems, nc, "skr0"), Ctr(sems, nc, "skr1")]
    sog = Ctr(sems, nc, "sog")
    srow = Ctr(sems, nc, "srow")
    gq = Ctr(sems, nc, "gq")
    pe = Ctr(sems, nc, "pe")
    dv = Ctr(sems, nc, "dv")
    ac = Ctr(sems, nc, "ac")
    po = Ctr(sems, nc, "po")

    sch = Sched()

    @sch.on("sync")
    def _(e):
        e.dma_start(out=conv_w[:], in_=conv_in).then_inc(spre.h, 16)
        e.dma_start(out=mi_sb[:], in_=mI_in).then_inc(spre.h, 16)
        e.dma_start(out=ms_sb[:], in_=mS_in).then_inc(spre.h, 16)
        e.dma_start(out=i_sb[:], in_=I_in).then_inc(spre.h, 16)
        e.dma_start(out=u_sb[:], in_=U_in).then_inc(spre.h, 16)
        e.dma_start(out=l_sb[:], in_=L_in).then_inc(spre.h, 16)
        e.dma_start(out=or_sb[:], in_=or_in).then_inc(spre.h, 16)
        e.dma_start(out=oc_sb[:], in_=oc_in).then_inc(spre.h, 16)
        e.dma_start(out=nw_sb[:], in_=nw_in).then_inc(spre.h, 16)
        e.dma_start(out=dtb_row[:], in_=dtb_r).then_inc(spre.h, 16)
        e.dma_start(out=nae_row[:], in_=alog_r).then_inc(spre.h, 16)
    spre.n += 11 * 16
    PRE_SY = spre.n

    @sch.on("gpsimd")
    def _(e):
        e.dma_start(out=wqkv[:], in_=wqkv_p).then_inc(gq.h, 16)
        e.dma_start(out=wzab[:], in_=wzab_p).then_inc(gq.h, 16)
        e.dma_start(out=idb[:], in_=I_in).then_inc(gq.h, 16)
    gq.n += 48
    PRE_GQ = gq.n

    @sch.on("vector")
    def _(e):
        e.memset(st_f[:], 0.0)
        e.memset(st_b[:], 0.0)
        e.memset(halo[:], 0.0).then_inc(dv.h, 1)
    dv.n += 1

    @sch.on("scalar")
    def _(e, w=PRE_SY):
        e.wait_ge(spre.h, w)
        e.activation(nae_row[:], nae_row[:], AF.Exp).then_inc(ac.h, 1)
    ac.n += 1
    _w = ac.n

    @sch.on("vector")
    def _(e, w=_w):
        e.wait_ge(ac.h, w)
        e.tensor_scalar_mul(nae_row[:], nae_row[:], -1.0).then_inc(dv.h, 1)
    dv.n += 1
    _w = dv.n

    @sch.on("tensor")
    def _(e, w=_w, w2=PRE_SY):
        e.wait_ge(dv.h, w)
        e.wait_ge(spre.h, w2)
        e.matmul(P2[:, 0:4], or_sb[0:1, :], nae_row[:], start=True, stop=True)
        e.matmul(P2[:, 4:8], or_sb[0:1, :], dtb_row[:], start=False, stop=True,
                 skip_group_check=True)
        e.fin(pe.h, 1)
    pe.n += 1
    _w = pe.n

    @sch.on("vector")
    def _(e, w=_w):
        e.wait_ge(pe.h, w)
        e.tensor_copy(nae_b[:], P2[:, 0:4])
        e.tensor_copy(dtb_b[:], P2[:, 4:8]).then_inc(dv.h, 1)
    dv.n += 1
    PRE_BC = dv.n

    ln_wr = []
    xt_free = [0, 0]
    xnt_free = [0, 0]

    def emit_ln_chunk(i):
        par = i % 2
        t0 = i * 128

        @sch.on("sync")
        def _(e, t0=t0, par=par, w=xt_free[par]):
            if w:
                e.wait_ge(dv.h, w)
            e.dma_start(out=xt[par][:],
                        in_=x_tok[t0:t0 + 128, :]).then_inc(sx[par].h, 16)
        sx[par].n += 16
        ld_n = sx[par].n

        @sch.on("scalar")
        def _(e, w=ld_n, w2=xnt_free[par], par=par):
            e.wait_ge(sx[par].h, w)
            if w2:
                e.wait_ge(sxw[par].h, w2)
            e.activation(xnt[par][:], xt[par][:], AF.Square,
                         accum_out=lnst[:, 0:1]).then_inc(ac.h, 1)
        ac.n += 1
        sq_n = ac.n

        @sch.on("vector")
        def _(e, w=ld_n, w2=sq_n, par=par):
            e.wait_ge(sx[par].h, w)
            e.tensor_reduce(lnst[:, 1:2], xt[par][:], AX.X, ALU.add)
            e.wait_ge(ac.h, w2)
            e.tensor_scalar_mul(lnst[:, 2:3], lnst[:, 1:2], 1.0 / D)
            e.tensor_tensor(lnst[:, 3:4], lnst[:, 2:3], lnst[:, 2:3], ALU.mult)
            e.tensor_scalar(lnst[:, 4:5], lnst[:, 0:1], 1.0 / D, None, ALU.mult)
            e.tensor_tensor(lnst[:, 4:5], lnst[:, 4:5], lnst[:, 3:4], ALU.subtract)
            e.tensor_scalar(lnst[:, 4:5], lnst[:, 4:5], LN_EPS, None,
                            ALU.add).then_inc(dv.h, 1)
        dv.n += 1
        va_n = dv.n

        @sch.on("scalar")
        def _(e, w=va_n):
            e.wait_ge(dv.h, w)
            e.activation(lnst[:, 5:6], lnst[:, 4:5], AF.Ln)
            e.activation(lnst[:, 6:7], lnst[:, 5:6], AF.Exp,
                         scale=np.float32(-0.5)).then_inc(ac.h, 1)
        ac.n += 1
        rs_n = ac.n

        @sch.on("vector")
        def _(e, w=rs_n, par=par):
            e.wait_ge(ac.h, w)
            e.tensor_scalar(xnt[par][:], xt[par][:], lnst[:, 2:3],
                            lnst[:, 6:7], ALU.subtract, ALU.mult).then_inc(dv.h, 1)
        dv.n += 1
        xt_free[par] = dv.n
        nm_n = dv.n

        @sch.on("sync")
        def _(e, w=nm_n, t0=t0, par=par):
            e.wait_ge(dv.h, w)
            e.dma_start(out=xn_d[t0:t0 + 128, :],
                        in_=xnt[par][:]).then_inc(sxw[par].h, 16)
        sxw[par].n += 16
        xnt_free[par] = sxw[par].n
        ln_wr.append((sxw[0].n, sxw[1].n))

    for i in range(min(8, 4 * nt)):
        emit_ln_chunk(i)

    rel_pe = [0] * (NT + 4)
    rel_po = [0] * (NT + 4)
    rel_dv = [0] * (NT + 4)
    prev_ab_pe = 0
    prev_scn = 0
    prev_o1 = 0
    prev_tile_atx = 0
    prev_sol_ac = 0

    for t in range(nt):
        par = t % 2
        for i in range(4 * (t + 2), min(4 * (t + 3), 4 * nt)):
            emit_ln_chunk(i)

        need_ln = ln_wr[4 * t + 3]

        @sch.on("sync")
        def _(e, t=t, par=par, w=need_ln, w2=prev_ab_pe):
            if w[0]:
                e.wait_ge(sxw[0].h, w[0])
            if w[1]:
                e.wait_ge(sxw[1].h, w[1])
            if w2:
                e.wait_ge(pe.h, w2)
            for kc in range(16):
                e.dma_start(out=xnf[:, kc * TILE:(kc + 1) * TILE],
                            in_=xn_d[t * TILE:(t + 1) * TILE,
                                     kc * 128:(kc + 1) * 128],
                            transpose=True).then_inc(sxr.h, 16)
        sxr.n += 256
        xnf_n = sxr.n

        evacs_q = [0, 0]
        for g in range(8):
            prev_evac = evacs_q[g % 2]
            @sch.on("tensor")
            def _(e, t=t, par=par, g=g, w=xnf_n, w2=prev_evac, w3=rel_po[t],
                  w4=prev_scn, w5=prev_sol_ac):
                if g == 0:
                    e.wait_ge(sxr.h, w)
                    if t == 0:
                        e.wait_ge(gq.h, PRE_GQ)
                    if w3:
                        e.wait_ge(po.h, w3)
                    if w4:
                        e.wait_ge(dv.h, w4)
                    if w5:
                        e.wait_ge(ac.h, w5)
                else:
                    if w2:
                        e.wait_ge(dv.h, w2)
                    if g == 1 and w4:
                        e.wait_ge(dv.h, w4)
                bank = PJ if g % 2 == 0 else B3
                for kc in range(16):
                    e.matmul(bank[:], wqkv[:, g * 2048 + kc * 128:
                                           g * 2048 + (kc + 1) * 128],
                             xnf[:, kc * TILE:(kc + 1) * TILE],
                             start=(kc == 0), stop=(kc == 15))
                e.fin(pe.h, 1)
            pe.n += 1
            mm_n = pe.n

            @sch.on("vector")
            def _(e, g=g, w=mm_n, w2=rel_po[t]):
                base = g * (TILE + 4)
                e.wait_ge(pe.h, w)
                if w2:
                    e.wait_ge(po.h, w2)
                e.tensor_copy(qkvr[:, base:base + 4], halo[:, g * 4:(g + 1) * 4])
                e.tensor_copy(qkvr[:, base + 4:base + 4 + TILE],
                              (PJ if g % 2 == 0 else B3)[:])
                e.tensor_copy(halo[:, g * 4:(g + 1) * 4],
                              qkvr[:, base + TILE:base + TILE + 4]).then_inc(dv.h, 1)
            dv.n += 1
            evacs_q[g % 2] = dv.n
            evac_g = dv.n

            @sch.on("vector")
            def _(e, g=g, w=evac_g, t=t, w2=rel_pe[t], w3=ac.n):
                base = g * (TILE + 4)
                e.wait_ge(dv.h, w)
                e.wait_ge(ac.h, w3)
                if t == 0 and g == 0:
                    e.wait_ge(spre.h, PRE_SY)
                if w2 and g == 0:
                    e.wait_ge(pe.h, w2)
                e.tensor_scalar(convt[:], qkvr[:, base + 1:base + 1 + TILE],
                                conv_w[:, g * 4:g * 4 + 1], None, ALU.mult)
                for k in range(1, 4):
                    e.scalar_tensor_tensor(convt[:],
                                           qkvr[:, base + 1 + k:base + 1 + k + TILE],
                                           conv_w[:, g * 4 + k:g * 4 + k + 1],
                                           convt[:], ALU.mult, ALU.add)
                e.fin(dv.h, 1)
            dv.n += 1
            cv_n = dv.n

            @sch.on("scalar")
            def _(e, g=g, w=cv_n, par=par):
                e.wait_ge(dv.h, w)
                e.activation(mixed[par][:, g * TILE:(g + 1) * TILE],
                             convt[:], AF.Silu).then_inc(ac.h, 1)
            ac.n += 1

            if g >= 2:
                @sch.on("sync")
                def _(e, par=par, g=g, w=ac.n, w2=rel_dv[t]):
                    e.wait_ge(ac.h, w)
                    if g == 2 and w2:
                        e.wait_ge(dv.h, w2)
                    gi = g - 2
                    e.dma_start(out=kv_d[par, gi * 128:(gi + 1) * 128, :],
                                in_=mixed[par][:, g * TILE:(g + 1) * TILE]
                                ).then_inc(skw.h, 16)
                skw.n += 16
                wr_n = skw.n

                if g == 3:
                    @sch.on("sync")
                    def _(e, par=par, w=wr_n):
                        e.wait_ge(skw.h, w)
                        for c in range(CPT):
                            e.dma_start(out=ktok[par][:, c * 256:(c + 1) * 256],
                                        in_=kv_d[par, 0:256,
                                                 c * 128:(c + 1) * 128],
                                        transpose=True).then_inc(skr[par].h, 16)
                    skr[par].n += 4 * 16
                elif g >= 4:
                    hh = g - 4

                    @sch.on("sync")
                    def _(e, par=par, hh=hh, w=wr_n):
                        e.wait_ge(skw.h, w)
                        for c in range(CPT):
                            e.dma_start(
                                out=vtok[par][:, c * 512 + hh * 128:
                                              c * 512 + (hh + 1) * 128],
                                in_=kv_d[par, 256 + hh * 128:
                                         256 + (hh + 1) * 128,
                                         c * 128:(c + 1) * 128],
                                transpose=True).then_inc(skr[par].h, 16)
                    skr[par].n += 4 * 16
        mixed_ac = ac.n
        kvr_n = skr[par].n

        prev_z = 0
        for c in range(CPT):
            @sch.on("tensor")
            def _(e, par=par, c=c, w=prev_z, w2=rel_po[t], w4=prev_scn,
                  w5=prev_sol_ac):
                if w:
                    e.wait_ge(dv.h, w)
                if c == 0 and w2:
                    e.wait_ge(po.h, w2)
                if c == 0 and w4:
                    e.wait_ge(dv.h, w4)
                if c == 0 and w5:
                    e.wait_ge(ac.h, w5)
                for kc in range(16):
                    lhs = xnf[:, kc * TILE + c * 128: kc * TILE + (c + 1) * 128]
                    e.matmul(PZ[:], lhs, wzab[:, kc * 520:kc * 520 + 512],
                             start=(kc == 0), stop=(kc == 15))
                e.fin(pe.h, 1)
            pe.n += 1
            zmm_n = pe.n

            @sch.on("vector")
            def _(e, c=c, par=par, w=zmm_n, w2=rel_po[t]):
                e.wait_ge(pe.h, w)
                if c == 0 and w2:
                    e.wait_ge(po.h, w2)
                e.tensor_copy(zt[par][:, c * 512:(c + 1) * 512],
                              PZ[:]).then_inc(dv.h, 1)
            dv.n += 1
            prev_z = dv.n
        z_dv = dv.n

        @sch.on("scalar")
        def _(e, par=par, w=z_dv, w2=po.n):
            e.wait_ge(dv.h, w)
            if w2:
                e.wait_ge(po.h, w2)
            for c in range(CPT):
                e.activation(zsil4[:, c * 512:(c + 1) * 512],
                             zt[par][:, c * 512:(c + 1) * 512],
                             AF.Silu).then_inc(ac.h, 1)
        ac.n += CPT
        zsil_ac = ac.n

        @sch.on("tensor")
        def _(e, par=par, w=z_dv):
            e.wait_ge(dv.h, w)
            for c in range(CPT):
                for kc in range(16):
                    lhs = xnf[:, kc * TILE + c * 128: kc * TILE + (c + 1) * 128]
                    e.matmul(PZ[:, c * 8:(c + 1) * 8], lhs,
                             wzab[:, kc * 520 + 512:(kc + 1) * 520],
                             start=(kc == 0 and c == 0), stop=(kc == 15),
                             skip_group_check=True)
            e.fin(pe.h, 1)
        pe.n += 1
        ab_pe = pe.n
        prev_ab_pe = ab_pe

        @sch.on("vector")
        def _(e, w=ab_pe):
            e.wait_ge(pe.h, w)
            e.tensor_copy(ab_sb[:], PZ[:, 0:32]).then_inc(dv.h, 1)
        dv.n += 1
        ab_dv = dv.n

        @sch.on("scalar")
        def _(e, w=ab_dv, w2=PRE_BC, w3=zsil_ac):
            e.wait_ge(dv.h, max(w, w2))
            e.wait_ge(ac.h, w3)
            for c in range(CPT):
                e.activation(pl16[:, c * 4:(c + 1) * 4],
                             ab_sb[:, c * 8 + 4:c * 8 + 8], AF.Abs)
            e.fin(ac.h, 1)
        ac.n += 1
        babs_ac = ac.n

        @sch.on("scalar")
        def _(e, w=babs_ac):
            e.activation(pl16[:], pl16[:], AF.Exp, scale=np.float32(-1.0))
            e.activation(pl16[:], pl16[:], AF.Ln,
                         bias=np.float32(1.0)).then_inc(ac.h, 1)
        ac.n += 1
        bl_ac = ac.n

        @sch.on("vector")
        def _(e, w=bl_ac):
            e.wait_ge(ac.h, w)
            for c in range(CPT):
                e.tensor_scalar(bpl[:, c * 4:(c + 1) * 4],
                                ab_sb[:, c * 8 + 4:c * 8 + 8], -1.0, 0.0,
                                ALU.mult, ALU.max)
            e.tensor_tensor(bpl[:], bpl[:], pl16[:], ALU.add).then_inc(dv.h, 1)
        dv.n += 1
        bsp_dv = dv.n

        @sch.on("scalar")
        def _(e, w=bsp_dv):
            e.wait_ge(dv.h, w)
            e.activation(bpl[:], bpl[:], AF.Exp,
                         scale=np.float32(-1.0)).then_inc(ac.h, 1)
        ac.n += 1
        bpl_dv = dv.n
        bpl_ac = ac.n

        for c in range(CPT):
            @sch.on("vector")
            def _(e, c=c, w=ab_dv, w2=bpl_ac):
                e.wait_ge(dv.h, w)
                e.wait_ge(ac.h, w2)
                e.tensor_tensor(sp4[:], ab_sb[:, c * 8:c * 8 + 4], dtb_b[:],
                                ALU.add).then_inc(dv.h, 1)
            dv.n += 1
            t1_dv = dv.n

            @sch.on("scalar")
            def _(e, c=c, w=t1_dv):
                e.wait_ge(dv.h, w)
                e.activation(pl16[:, c * 4:(c + 1) * 4], sp4[:], AF.Abs)
                e.activation(pl16[:, c * 4:(c + 1) * 4],
                             pl16[:, c * 4:(c + 1) * 4], AF.Exp,
                             scale=np.float32(-1.0))
                e.activation(pl16[:, c * 4:(c + 1) * 4],
                             pl16[:, c * 4:(c + 1) * 4], AF.Ln,
                             bias=np.float32(1.0)).then_inc(ac.h, 1)
            ac.n += 1
            l1p_ac = ac.n

            @sch.on("vector")
            def _(e, c=c, w=l1p_ac):
                e.wait_ge(ac.h, w)
                e.tensor_scalar_max(sp4[:], sp4[:], 0.0)
                e.tensor_tensor(sp4[:], sp4[:], pl16[:, c * 4:(c + 1) * 4], ALU.add)
                e.tensor_tensor(gpl[:, c * 4:(c + 1) * 4], sp4[:], nae_b[:],
                                ALU.mult).then_inc(dv.h, 1)
            dv.n += 1
        gpl_dv = dv.n

        @sch.on("scalar")
        def _(e, par=par):
            e.activation(ksq[:], mixed[par][:, 2 * TILE:3 * TILE],
                         AF.Square).then_inc(ac.h, 1)
        ac.n += 1
        k0_ac = ac.n

        @sch.on("tensor")
        def _(e, w=gpl_dv, w2=k0_ac):
            e.wait_ge(dv.h, w)
            e.wait_ge(ac.h, w2)
            e.matmul(P2[:, 0:16], u_sb[:], gpl[:], start=True, stop=True)
            e.matmul(P2[:, 16:32], l_sb[:], gpl[:], start=False, stop=True,
                     skip_group_check=True)
            for c in range(CPT):
                e.matmul(P2[:, 32 + c * 2:33 + c * 2],
                         ksq[:, c * 128:(c + 1) * 128], oc_sb[:],
                         start=False, stop=True, skip_group_check=True)
            e.fin(pe.h, 1)
        pe.n += 1
        n0_pe = pe.n

        @sch.on("scalar")
        def _(e, par=par, w=n0_pe):
            e.wait_ge(pe.h, w)
            e.activation(ksq[:], mixed[par][:, 3 * TILE:4 * TILE],
                         AF.Square).then_inc(ac.h, 1)
        ac.n += 1
        k1_ac = ac.n

        @sch.on("tensor")
        def _(e, w=k1_ac):
            e.wait_ge(ac.h, w)
            for c in range(CPT):
                e.matmul(P2[:, 33 + c * 2:34 + c * 2],
                         ksq[:, c * 128:(c + 1) * 128], oc_sb[:],
                         start=False, stop=True, skip_group_check=True)
            e.fin(pe.h, 1)
        pe.n += 1
        n1_pe = pe.n

        @sch.on("scalar")
        def _(e, par=par, w=n1_pe):
            e.wait_ge(pe.h, w)
            e.activation(ksq[:], mixed[par][:, 0:TILE],
                         AF.Square).then_inc(ac.h, 1)
        ac.n += 1
        q0_ac = ac.n

        @sch.on("tensor")
        def _(e, w=q0_ac):
            e.wait_ge(ac.h, w)
            for c in range(CPT):
                e.matmul(P2[:, 40 + c * 2:41 + c * 2],
                         ksq[:, c * 128:(c + 1) * 128], oc_sb[:],
                         start=False, stop=True, skip_group_check=True)
            e.fin(pe.h, 1)
        pe.n += 1
        n2_pe = pe.n

        @sch.on("scalar")
        def _(e, par=par, w=n2_pe):
            e.wait_ge(pe.h, w)
            e.activation(ksq[:], mixed[par][:, TILE:2 * TILE],
                         AF.Square).then_inc(ac.h, 1)
        ac.n += 1
        q1_ac = ac.n

        @sch.on("tensor")
        def _(e, w=q1_ac):
            e.wait_ge(ac.h, w)
            for c in range(CPT):
                e.matmul(P2[:, 41 + c * 2:42 + c * 2],
                         ksq[:, c * 128:(c + 1) * 128], oc_sb[:],
                         start=False, stop=True, skip_group_check=True)
            e.fin(pe.h, 1)
        pe.n += 1
        pln_pe = pe.n

        @sch.on("vector")
        def _(e, w=pln_pe):
            e.wait_ge(pe.h, w)
            e.tensor_copy(cgc[:], P2[:, 0:16])
            e.tensor_copy(rvc[:], P2[:, 16:32]).then_inc(dv.h, 1)
        dv.n += 1
        cg_dv = dv.n

        @sch.on("scalar")
        def _(e, w=pln_pe):
            e.wait_ge(pe.h, w)
            e.activation(lnssq[:], P2[:, 32:48], AF.Ln).then_inc(ac.h, 1)
        ac.n += 1
        lnq_ac = ac.n

        @sch.on("vector")
        def _(e, w=lnq_ac):
            e.wait_ge(ac.h, w)
            e.tensor_scalar(lrkp[:], lnssq[:, 0:8], -0.5, LNCLAMP,
                            ALU.mult, ALU.min)
            e.tensor_scalar(lrqp[:], lnssq[:, 8:16], -0.5, LNCLAMP,
                            ALU.mult, ALU.min)
            e.tensor_scalar(lrqp[:], lrqp[:], S_SCALE, None, ALU.add).then_inc(dv.h, 1)
        dv.n += 1
        lr_dv = dv.n

        @sch.on("scalar")
        def _(e, w=lr_dv):
            e.wait_ge(dv.h, w)
            e.activation(rkp[:], lrkp[:], AF.Exp).then_inc(ac.h, 1)
        ac.n += 1
        rk_ac = ac.n

        @sch.on("vector")
        def _(e, w=rk_ac, w2=cg_dv):
            e.wait_ge(ac.h, w)
            e.wait_ge(dv.h, w2)
            lrk4 = lrkp[:].rearrange("p (c j) -> p c j", c=4)
            lrq4 = lrqp[:].rearrange("p (c j) -> p c j", c=4)
            rk4 = rkp[:].rearrange("p (c j) -> p c j", c=4)
            pla4 = pla[:].rearrange("p (c h) -> p c h", c=4)
            plw4 = plw[:].rearrange("p (c h) -> p c h", c=4)
            pl4 = pl16[:].rearrange("p (c h) -> p c h", c=4)
            for r in range(2):
                e.tensor_copy(pla4[:, :, r::2], lrk4)
                e.tensor_copy(plw4[:, :, r::2], lrq4)
                e.tensor_copy(pl4[:, :, r::2], rk4)
            e.tensor_tensor(brk[:], bpl[:], pl16[:], ALU.mult)
            e.tensor_tensor(pla[:], pla[:], cgc[:], ALU.add)
            e.tensor_tensor(plw[:], plw[:], cgc[:], ALU.add).then_inc(dv.h, 1)
        dv.n += 1
        plm_dv = dv.n

        @sch.on("scalar")
        def _(e, w=plm_dv):
            e.wait_ge(dv.h, w)
            e.activation(gban[:], pla[:], AF.Exp)
            e.activation(gqa[:], plw[:], AF.Exp)
            e.activation(cpl[:], rvc[:], AF.Exp).then_inc(ac.h, 1)
        ac.n += 1
        gsc_ac = ac.n

        @sch.on("vector")
        def _(e, w=gsc_ac):
            e.wait_ge(ac.h, w)
            e.tensor_scalar_mul(gban[:], gban[:], -1.0)
            e.tensor_tensor(pl16[:], cgc[:], rvc[:], ALU.add)
            e.tensor_tensor(cpl[:], cpl[:], bpl[:], ALU.mult).then_inc(dv.h, 1)
        dv.n += 1
        gam_dv = dv.n

        @sch.on("scalar")
        def _(e, w=gam_dv):
            e.wait_ge(dv.h, w)
            e.activation(gam[:], pl16[:], AF.Exp).then_inc(ac.h, 1)
        ac.n += 1
        gam_ac = ac.n

        @sch.on("tensor")
        def _(e, w=plm_dv):
            e.wait_ge(dv.h, w)
            e.matmul(B5[0:16, 0:128], pla[:], i_sb[:], start=True, stop=True)
            e.matmul(B5[0:16, 128:256], plw[:], i_sb[:], start=False, stop=True,
                     skip_group_check=True)
            e.fin(pe.h, 1)
        pe.n += 1
        rows_pe = pe.n

        @sch.on("vector")
        def _(e, w=rows_pe):
            e.wait_ge(pe.h, w)
            e.tensor_copy(rowA[:], B5[0:16, 0:128])
            e.tensor_copy(rowW[:], B5[0:16, 128:256]).then_inc(dv.h, 1)
        dv.n += 1
        rows_dv = dv.n

        @sch.on("gpsimd")
        def _(e, par=par, w=rows_dv):
            e.wait_ge(dv.h, w)
            e.dma_start(out=row_d[par, 0].rearrange("(a b) -> a b", a=16),
                        in_=rowA[:]).then_inc(srow.h, 16)
            e.dma_start(out=row_d[par, 1].rearrange("(a b) -> a b", a=16),
                        in_=rowW[:]).then_inc(srow.h, 16)
        srow.n += 32
        roww_n = srow.n

        @sch.on("gpsimd")
        def _(e, par=par, w=roww_n):
            e.wait_ge(srow.h, w)
            e.dma_start(out=flat[par * 32:par * 32 + 1, :],
                        in_=row_d[par].rearrange("a b -> (a b)"
                                                 ).rearrange("(a b) -> a b", a=1)
                        ).then_inc(srow.h, 16)
        srow.n += 16
        flat_n = srow.n

        atp = at_bf
        prep_awb = []
        prep_atx = []
        prev_dbc = 0
        for c in range(CPT):
            @sch.on("tensor")
            def _(e, par=par, c=c, w=flat_n, w2=gam_ac, w3=kvr_n,
                  w4=(prep_awb[c - 1][1] if c else dv.n)):
                e.wait_ge(srow.h, w)
                if c == 0:
                    e.wait_ge(ac.h, w2)
                    e.wait_ge(skr[par].h, w3)
                    e.wait_ge(dv.h, w4)
                else:
                    e.wait_ge(dv.h, w4)
                for j in range(2):
                    kf = mixed[par][:, (2 + j) * TILE + c * 128:
                                    (2 + j) * TILE + (c + 1) * 128]
                    qf = mixed[par][:, j * TILE + c * 128:
                                    j * TILE + (c + 1) * 128]
                    e.matmul(B3[:, j * 256:j * 256 + 128], kf, kf,
                             start=(j == 0), stop=True, skip_group_check=True)
                    e.matmul(B3[:, j * 256 + 128:j * 256 + 256], kf, qf,
                             start=False, stop=True, skip_group_check=True)
                e.matmul(B5[:], or_sb[par * 32:par * 32 + 1, :],
                         flat[par * 32:par * 32 + 1, c * 512:(c + 1) * 512],
                         start=True, stop=True, skip_group_check=True)
                e.matmul(B6[:], or_sb[par * 32:par * 32 + 1, :],
                         flat[par * 32:par * 32 + 1,
                              2048 + c * 512:2048 + (c + 1) * 512],
                         start=True, stop=True, skip_group_check=True)
                e.fin(pe.h, 1)
            pe.n += 1
            pp_pe = pe.n

            @sch.on("vector")
            def _(e, c=c, w=pp_pe):
                e.wait_ge(pe.h, w)
                for h in range(4):
                    e.scalar_tensor_tensor(
                        dmt[:, h * 128:(h + 1) * 128],
                        B5[:, h * 128:(h + 1) * 128],
                        cgc[:, c * 4 + h:c * 4 + h + 1],
                        ms_sb[:], ALU.subtract, ALU.add)
                for h in range(4):
                    e.scalar_tensor_tensor(
                        dmt2[:, h * 128:(h + 1) * 128],
                        B6[:, h * 128:(h + 1) * 128],
                        cgc[:, c * 4 + h:c * 4 + h + 1],
                        mi_sb[:], ALU.subtract, ALU.add)
                e.fin(dv.h, 1)
            dv.n += 1
            dm_dv = dv.n

            @sch.on("scalar")
            def _(e, w=dm_dv):
                e.wait_ge(dv.h, w)
                e.activation(exa[:], dmt[:], AF.Exp)
                e.activation(exw[:], dmt2[:], AF.Exp).then_inc(ac.h, 1)
            ac.n += 1
            ex_ac = ac.n

            @sch.on("vector")
            def _(e, c=c, par=par, w=pp_pe, w2=ex_ac):
                e.wait_ge(pe.h, w)
                e.wait_ge(ac.h, w2)
                for h in range(4):
                    j = h // 2
                    e.scalar_tensor_tensor(
                        at_bf[:, c * 512 + h * 128:c * 512 + (h + 1) * 128],
                        B3[:, j * 256:j * 256 + 128],
                        brk[:, c * 4 + h:c * 4 + h + 1],
                        exa[:, h * 128:(h + 1) * 128], ALU.mult, ALU.mult)
                    e.scalar_tensor_tensor(
                        wt_bf[:, c * 512 + h * 128:c * 512 + (h + 1) * 128],
                        B3[:, j * 256 + 128:j * 256 + 256],
                        brk[:, c * 4 + h:c * 4 + h + 1],
                        exw[:, h * 128:(h + 1) * 128], ALU.mult, ALU.mult)
                e.fin(dv.h, 1)
            dv.n += 1
            prep_awb.append((None, dv.n))
            aw_dv = dv.n

            @sch.on("tensor")
            def _(e, c=c, w=aw_dv, w2=(prep_atx[c - 1] if c else prev_tile_atx)):
                e.wait_ge(dv.h, w)
                if w2:
                    e.wait_ge(ac.h, w2)
                for h in range(4):
                    e.transpose(XB[:, h * 128:(h + 1) * 128],
                                at_bf[:, c * 512 + h * 128:
                                      c * 512 + (h + 1) * 128], idb[:])
                e.fin(pe.h, 1)
            pe.n += 1
            tr_pe = pe.n

            @sch.on("scalar")
            def _(e, c=c, w=tr_pe):
                e.wait_ge(pe.h, w)
                e.copy(a_bf[:, c * 512:(c + 1) * 512], XB[:]).then_inc(ac.h, 1)
            ac.n += 1
            prep_atx.append(ac.n)
        @sch.on("vector")
        def _(e, par=par, w=kvr_n, w2=rk_ac, w3=prev_scn):
            e.wait_ge(skr[par].h, w)
            e.wait_ge(ac.h, w2)
            if w3:
                e.wait_ge(dv.h, w3)
            for c2 in range(CPT):
                for j in range(2):
                    e.tensor_scalar(kh_bf[:, c2 * 256 + j * 128:
                                          c2 * 256 + (j + 1) * 128],
                                    ktok[par][:, c2 * 256 + j * 128:
                                              c2 * 256 + (j + 1) * 128],
                                    rkp[:, c2 * 2 + j:c2 * 2 + j + 1], None,
                                    ALU.mult)
            e.fin(dv.h, 1)
        dv.n += 1

        prep_done_dv = dv.n
        prep_done_ac = ac.n
        prev_tile_atx = prep_atx[-1]

        for c in range(CPT):
            @sch.on("tensor")
            def _(e, par=par, c=c, w=prep_done_dv, w2=prev_scn, w3=prev_o1):
                if c == 0:
                    e.wait_ge(dv.h, w)
                if w2:
                    e.wait_ge(dv.h, max(w2, w3))
                for j in range(2):
                    kf = mixed[par][:, (2 + j) * TILE + c * 128:
                                    (2 + j) * TILE + (c + 1) * 128]
                    qf = mixed[par][:, j * TILE + c * 128:
                                    j * TILE + (c + 1) * 128]
                    e.matmul(B4[:, j * 256:(j + 1) * 256], kf,
                             st_b[:, j * 256:(j + 1) * 256],
                             start=(j == 0), stop=True, skip_group_check=True)
                    e.matmul(P2[:, j * 256:(j + 1) * 256], qf,
                             st_b[:, j * 256:(j + 1) * 256],
                             start=(j == 0), stop=True, skip_group_check=True)
                e.fin(pe.h, 1)
            pe.n += 1
            bld_pe = pe.n

            @sch.on("vector")
            def _(e, c=c, par=par, w=bld_pe):
                e.wait_ge(pe.h, w)
                for h in range(4):
                    e.scalar_tensor_tensor(
                        b_bf[:, h * 128:(h + 1) * 128],
                        B4[:, (h // 2) * 256 + (h % 2) * 128:
                           (h // 2) * 256 + (h % 2 + 1) * 128],
                        gban[:, c * 4 + h:c * 4 + h + 1],
                        vtok[par][:, c * 512 + h * 128:c * 512 + (h + 1) * 128],
                        ALU.mult, ALU.add)
                e.fin(dv.h, 1)
            dv.n += 1
            b_dv = dv.n

            @sch.on("tensor")
            def _(e, c=c, w=b_dv):
                e.wait_ge(dv.h, w)
                for h in range(4):
                    e.matmul(PZ[:, h * 128:(h + 1) * 128],
                             at_bf[:, c * 512 + h * 128:c * 512 + (h + 1) * 128],
                             b_bf[:, h * 128:(h + 1) * 128],
                             start=(h == 0), stop=True, skip_group_check=True)
                e.fin(pe.h, 1)
            pe.n += 1
            ff_pe = pe.n

            @sch.on("vector")
            def _(e, w=ff_pe):
                e.wait_ge(pe.h, w)
                e.tensor_tensor(x_bf[:], b_bf[:], PZ[:], ALU.subtract)
                e.fin(dv.h, 1)
            dv.n += 1
            x_dv = dv.n

            xs = [x_bf, b_bf]
            for lvl in range(1, 7):
                if lvl == 1:
                    at_s = lambda h, c=c: at_bf[:, c * 512 + h * 128:
                                                c * 512 + (h + 1) * 128]
                    a_s = lambda h, c=c: a_bf[:, c * 512 + h * 128:
                                              c * 512 + (h + 1) * 128]
                else:
                    ssrc = s2_bf if (lvl - 1) % 2 == 1 else s3_bf
                    at_s = lambda h, ssrc=ssrc: ssrc[:, h * 256:h * 256 + 128]
                    a_s = lambda h, ssrc=ssrc: ssrc[:, h * 256 + 128:h * 256 + 256]
                sdst = s2_bf if lvl % 2 == 1 else s3_bf
                at_d = lambda h, sdst=sdst: sdst[:, h * 256:h * 256 + 128]
                x_s = xs[(lvl - 1) % 2]
                x_d = xs[lvl % 2]

                @sch.on("tensor")
                def _(e, at_s=at_s, a_s=a_s, lvl=lvl, w=dv.n, w2=ac.n):
                    e.wait_ge(dv.h, w)
                    e.wait_ge(ac.h, w2)
                    for h in range(4):
                        o = (h % 2) * 256
                        bank = PZ if h < 2 else PJ
                        e.matmul(bank[:, o:o + 128], a_s(h), at_s(h),
                                 start=(h % 2 == 0), stop=True,
                                 skip_group_check=True)
                        if lvl < 6:
                            e.matmul(bank[:, o + 128:o + 256], at_s(h), a_s(h),
                                     start=False, stop=True,
                                     skip_group_check=True)
                    e.fin(pe.h, 1)
                pe.n += 1
                sq_pe = pe.n

                @sch.on("scalar")
                def _(e, sdst=sdst, lvl=lvl, w=sq_pe):
                    e.wait_ge(pe.h, w)
                    e.copy(sdst[:, 0:512], PZ[:])
                    e.copy(sdst[:, 512:1024], PJ[:])
                    e.fin(ac.h, 1)
                ac.n += 1
                sqe_ac = ac.n

                @sch.on("tensor")
                def _(e, at_d=at_d, x_s=x_s, w=sqe_ac):
                    e.wait_ge(ac.h, w)
                    for h in range(4):
                        o = (h % 2) * 128
                        bank = PZ if h < 2 else PJ
                        e.matmul(bank[:, o:o + 128], at_d(h),
                                 x_s[:, h * 128:(h + 1) * 128],
                                 start=(h % 2 == 0), stop=False,
                                 skip_group_check=True)
                        e.matmul(bank[:, o:o + 128], idb[:],
                                 x_s[:, h * 128:(h + 1) * 128],
                                 start=False, stop=True,
                                 skip_group_check=True)
                    e.fin(pe.h, 1)
                pe.n += 1
                ap_pe = pe.n

                @sch.on("scalar")
                def _(e, x_d=x_d, w=ap_pe):
                    e.wait_ge(pe.h, w)
                    e.copy(x_d[:, 0:256], PZ[:, 0:256])
                    e.copy(x_d[:, 256:512], PJ[:, 0:256])
                    e.fin(ac.h, 1)
                ac.n += 1
            sol_ac = ac.n
            prev_sol_ac = sol_ac
            XFIN = xs[0]

            @sch.on("vector")
            def _(e, c=c, par=par, w=sol_ac):
                e.wait_ge(ac.h, w)
                for h in range(4):
                    j = h // 2
                    e.tensor_scalar(o1[:, h * 128:(h + 1) * 128],
                                    P2[:, j * 256 + (h % 2) * 128:
                                       j * 256 + (h % 2 + 1) * 128],
                                    gqa[:, c * 4 + h:c * 4 + h + 1], None,
                                    ALU.mult)
                    e.tensor_scalar(dc_bf[:, h * 128:(h + 1) * 128],
                                    XFIN[:, h * 128:(h + 1) * 128],
                                    cpl[:, c * 4 + h:c * 4 + h + 1], None,
                                    ALU.mult)
                e.fin(dv.h, 1)
            dv.n += 1
            odk_dv = dv.n

            @sch.on("tensor")
            def _(e, c=c, w=odk_dv):
                e.wait_ge(dv.h, w)
                for h in range(4):
                    bank = PZ if h < 2 else PJ
                    o = (h % 2) * 256
                    e.matmul(bank[:, o:o + 128],
                             wt_bf[:, c * 512 + h * 128:c * 512 + (h + 1) * 128],
                             XFIN[:, h * 128:(h + 1) * 128],
                             start=(h % 2 == 0), stop=True,
                             skip_group_check=True)
                    e.matmul(bank[:, o + 128:o + 256],
                             kh_bf[:, c * 256 + (h // 2) * 128:
                                   c * 256 + (h // 2 + 1) * 128],
                             dc_bf[:, h * 128:(h + 1) * 128],
                             start=False, stop=True, skip_group_check=True)
                e.fin(pe.h, 1)
            pe.n += 1
            ws_pe = pe.n

            osb_sl = osb[:, (c % 2) * 512:(c % 2 + 1) * 512]

            @sch.on("vector")
            def _(e, c=c, w=ws_pe, osl=osb_sl, w2=rel_po[t] if False else po.n):
                e.wait_ge(pe.h, w)
                if w2:
                    e.wait_ge(po.h, w2)
                for h in range(4):
                    bank = PZ if h < 2 else PJ
                    o = (h % 2) * 256
                    e.tensor_tensor(osl[:, h * 128:(h + 1) * 128],
                                    o1[:, h * 128:(h + 1) * 128],
                                    bank[:, o:o + 128], ALU.add)
                    e.scalar_tensor_tensor(st_f[:, h * 128:(h + 1) * 128],
                                           st_f[:, h * 128:(h + 1) * 128],
                                           gam[:, c * 4 + h:c * 4 + h + 1],
                                           bank[:, o + 128:o + 256],
                                           ALU.mult, ALU.add)
                    e.tensor_copy(st_b[:, h * 128:(h + 1) * 128],
                                  st_f[:, h * 128:(h + 1) * 128])
                e.fin(dv.h, 1)
            dv.n += 1
            scn_dv = dv.n
            prev_scn = scn_dv
            prev_o1 = scn_dv

            @sch.on("scalar")
            def _(e, w=scn_dv, osl=osb_sl):
                e.wait_ge(dv.h, w)
                e.activation(osq[:], osl, AF.Square).then_inc(ac.h, 1)
            ac.n += 1
            osq_ac = ac.n

            @sch.on("vector")
            def _(e, w=osq_ac):
                e.wait_ge(ac.h, w)
                e.tensor_reduce(rms[:, 0:4].rearrange("p (g o) -> p g o", o=1),
                                osq[:].rearrange("p (g d) -> p g d", g=4),
                                AX.X, ALU.add)
                e.fin(dv.h, 1)
            dv.n += 1
            rm_dv = dv.n

            @sch.on("scalar")
            def _(e, w=rm_dv, osl=osb_sl):
                e.wait_ge(dv.h, w)
                e.activation(rms[:, 4:8], rms[:, 0:4], AF.Ln,
                             scale=np.float32(1.0 / DV), bias=np.float32(EPS))
                e.activation(rms[:, 4:8], rms[:, 4:8], AF.Exp,
                             scale=np.float32(-0.5))
                for h in range(4):
                    e.mul(osl[:, h * 128:(h + 1) * 128],
                          osl[:, h * 128:(h + 1) * 128], rms[:, 4 + h:5 + h])
                e.fin(ac.h, 1)
            ac.n += 1
            zg_ac = ac.n

            @sch.on("gpsimd")
            def _(e, t=t, c=c, par=par, w=zg_ac, w2=sog.n, osl=osb_sl):
                e.wait_ge(ac.h, w)
                if w2:
                    e.wait_ge(sog.h, w2)
                e.tensor_tensor(osl, osl, nw_sb[:], ALU.mult)
                e.tensor_tensor(ogt[:], osl,
                                zsil4[:, c * 512:(c + 1) * 512], ALU.mult)
                e.fin(po.h, 1)
                e.dma_start(out=og[t * TILE + c * 128:t * TILE + (c + 1) * 128, :],
                            in_=ogt[:]).then_inc(sog.h, 16)
            po.n += 1
            sog.n += 16
        rel_pe[t + 2] = pe.n
        rel_po[t + 2] = po.n
        rel_dv[t + 2] = dv.n

    og_done = sog.n

    ccs = Ctr(sems, nc, "ccs")

    @sch.on("gpsimd")
    def _(e, w=og_done):
        e.wait_ge(sog.h, w)
        from concourse.aot_env import allow_ici_all_to_all
        with allow_ici_all_to_all():
            e.collective_compute(
                "AllToAll", ALU.bypass,
                replica_groups=[list(range(NCORE))],
                ins=[cc_in_h.ap().opt()],
                outs=[cc_out_h.ap().opt()],
            ).then_inc(ccs.h, 1)

    for eng_ in ENGINES:
        @sch.on(eng_)
        def _(e):
            e.wait_ge(ccs.h, 1)

    ctx.close()
    es2 = ExitStack()

    emit_phase2(nc, sch, sems, es2, x2, xT2, w1_p, w2_p, w3_p, wo_p,
                b1_in, b2_in, b3_in, outf, xn2_d, cc_out)

    chain_sems = {eng: sems.enter_context(nc.semaphore(f"ch_{eng}"))
                  for eng in ("vector", "scalar", "gpsimd")}
    with sems, es2, nc.Block() as block:
        for eng in ENGINES:
            steps = sch.streams[eng]

            def runner(e, steps=steps, ch=chain_sems.get(eng)):
                e = EngRec(e, chain=ch)
                for fn in steps:
                    fn(e)
            getattr(block, eng)(runner)
    return nc



def _l1_inputs(inputs):
    x = np.ascontiguousarray(
        np.asarray(inputs["L0_attn_hidden_states"], np.float32).reshape(S, D))
    W_qkv = np.asarray(inputs["L0_attn_W_qkv"], np.float32)
    W_z = np.asarray(inputs["L0_attn_W_z"], np.float32)
    W_a = np.asarray(inputs["L0_attn_W_a"], np.float32)
    W_b = np.asarray(inputs["L0_attn_W_b"], np.float32)
    conv_w = np.asarray(inputs["L0_attn_conv_w"], np.float32)
    A_log = np.asarray(inputs["L0_attn_A_log"], np.float32)
    dt_bias = np.asarray(inputs["L0_attn_dt_bias"], np.float32)
    norm_w = np.asarray(inputs["L0_attn_norm_w"], np.float32)

    mS = np.where(np.triu(np.ones((128, 128), np.float32), 1) > 0, 0., MASKVAL
                  ).astype(np.float32)
    mI = np.where(np.triu(np.ones((128, 128), np.float32), 0) > 0, 0., MASKVAL
                  ).astype(np.float32)
    I128 = np.eye(128, dtype=np.float32)
    U128 = np.triu(np.ones((128, 128), np.float32))
    L128 = np.tril(np.ones((128, 128), np.float32), -1)
    ones_r = np.ones((33, 128), np.float32)
    ones_c = np.ones((128, 1), np.float32)

    maps = []
    for c in range(NCORE):
        j0, j1 = 2 * c, 2 * c + 1
        hs = [4 * c + i for i in range(4)]
        rows = np.r_[np.arange(j0 * 128, (j0 + 1) * 128),
                     np.arange(j1 * 128, (j1 + 1) * 128),
                     np.arange(KEY + j0 * 128, KEY + (j0 + 1) * 128),
                     np.arange(KEY + j1 * 128, KEY + (j1 + 1) * 128)]
        vrows = np.concatenate([np.arange(2 * KEY + h * 128, 2 * KEY + (h + 1) * 128)
                                for h in hs])
        rows = np.r_[rows, vrows]
        Wsl = W_qkv[rows]
        wqkv_pp = _pretile(np.ascontiguousarray(Wsl.T), 16, 8)
        zrows = np.concatenate([np.arange(h * 128, (h + 1) * 128) for h in hs])
        Wz_sl = W_z[zrows]
        Wzab = np.concatenate([Wz_sl, W_a[hs], W_b[hs]], axis=0)
        ZT = np.ascontiguousarray(Wzab.T)
        wzab_pp = np.ascontiguousarray(
            ZT.reshape(16, 128, 520).transpose(1, 0, 2).reshape(128, 16 * 520))
        cw = conv_w[rows]
        conv_pp = np.ascontiguousarray(
            cw.reshape(8, 128, 4).transpose(1, 0, 2).reshape(128, 32))
        maps.append(dict(
            x_tok=x, wqkv_p=wqkv_pp, wzab_p=wzab_pp, conv_in=conv_pp,
            alog_r=np.ascontiguousarray(A_log[hs][None]),
            dtb_r=np.ascontiguousarray(dt_bias[hs][None]),
            nw_in=np.ascontiguousarray(
                np.broadcast_to(np.tile(norm_w, 4)[None], (128, 512))),
            mS_in=mS, mI_in=mI, I_in=I128, U_in=U128, L_in=L128,
            or_in=ones_r, oc_in=ones_c,
        ))
    return maps





TOK2 = S // NCORE
NKT2 = TOK2 // 512


def emit_phase2(nc, sch, sems, ctx, x2, xT2, w1_p, w2_p, w3_p, wo_p,
                b1_in, b2_in, b3_in, outf, xn2_d, cc_out):
    sb = lambda n_, sh, d_=F32: ctx.enter_context(nc.sbuf_tensor(n_, sh, d_))
    psa = lambda n_, sh, d_=F32: ctx.enter_context(nc.psum_tensor(n_, sh, d_))

    xt2 = sb("xt2", [128, D])
    xnt2 = sb("xnt2", [128, D], BF16)
    lnst2 = sb("lnst2", [128, 8])
    xnf2 = sb("xnf2", [128, 16 * TOK2], BF16)
    xTt = xnf2[:, 0:2 * TOK2].bitcast(F32)
    outt = xnf2[:, 2 * TOK2:4 * TOK2].bitcast(F32)
    h1f = sb("h1f", [128, 32 * TOK2], BF16)
    h2f = sb("h2f", [128, 32 * TOK2], BF16)
    oT = h1f
    wblk = [sb(f"wblk{i}", [128, 32 * 128], BF16) for i in range(2)]
    wblk2 = [sb(f"wblk2{i}", [128, 32 * 128], BF16) for i in range(2)]
    b1s = sb("b1s", [128, 32])
    b2s = sb("b2s", [128, 32])
    b3s = sb("b3s", [128, 16])


    PA = [psa(f"PA{i}", [128, 512]) for i in range(4)]

    sx2 = Ctr(sems, nc, "sx2")
    sxw2 = Ctr(sems, nc, "sxw2")
    sxr2 = Ctr(sems, nc, "sxr2")
    sor2 = Ctr(sems, nc, "sor2")
    swl = [Ctr(sems, nc, "swl0"), Ctr(sems, nc, "swl1")]
    sw2 = [Ctr(sems, nc, "sw2a"), Ctr(sems, nc, "sw2b")]
    sw3 = [Ctr(sems, nc, "sw3a"), Ctr(sems, nc, "sw3b")]
    sxt2 = Ctr(sems, nc, "sxt2")
    sot = Ctr(sems, nc, "sot")
    spre2 = Ctr(sems, nc, "spre2")
    pe = Ctr(sems, nc, "pe2")
    dv = Ctr(sems, nc, "dv2")
    ac = Ctr(sems, nc, "ac2")

    @sch.on("sync")
    def _(e):
        e.dma_start(out=b1s[:], in_=b1_in).then_inc(spre2.h, 16)
        e.dma_start(out=b2s[:], in_=b2_in).then_inc(spre2.h, 16)
        e.dma_start(out=b3s[:], in_=b3_in).then_inc(spre2.h, 16)
    spre2.n += 48
    PRE2 = spre2.n

    for i in range(TOK2 // 128):
        @sch.on("sync")
        def _(e, i=i, w=dv.n):
            if w:
                e.wait_ge(dv.h, w)
            e.dma_start(out=xt2[:], in_=x2[i * 128:(i + 1) * 128, :]
                        ).then_inc(sx2.h, 16)
        sx2.n += 16
        ld_n = sx2.n

        @sch.on("scalar")
        def _(e, w=ld_n, w2=sxw2.n):
            e.wait_ge(sx2.h, w)
            if w2:
                e.wait_ge(sxw2.h, w2)
            e.activation(xnt2[:], xt2[:], AF.Square,
                         accum_out=lnst2[:, 0:1]).then_inc(ac.h, 1)
        ac.n += 1
        sq_n = ac.n

        @sch.on("vector")
        def _(e, w=ld_n, w2=sq_n):
            e.wait_ge(sx2.h, w)
            e.tensor_reduce(lnst2[:, 1:2], xt2[:], AX.X, ALU.add)
            e.wait_ge(ac.h, w2)
            e.tensor_scalar_mul(lnst2[:, 2:3], lnst2[:, 1:2], 1.0 / D)
            e.tensor_tensor(lnst2[:, 3:4], lnst2[:, 2:3], lnst2[:, 2:3], ALU.mult)
            e.tensor_scalar(lnst2[:, 4:5], lnst2[:, 0:1], 1.0 / D, None, ALU.mult)
            e.tensor_tensor(lnst2[:, 4:5], lnst2[:, 4:5], lnst2[:, 3:4],
                            ALU.subtract)
            e.tensor_scalar(lnst2[:, 4:5], lnst2[:, 4:5], LN_EPS, None,
                            ALU.add).then_inc(dv.h, 1)
        dv.n += 1
        va_n = dv.n

        @sch.on("scalar")
        def _(e, w=va_n):
            e.wait_ge(dv.h, w)
            e.activation(lnst2[:, 5:6], lnst2[:, 4:5], AF.Ln)
            e.activation(lnst2[:, 6:7], lnst2[:, 5:6], AF.Exp,
                         scale=np.float32(-0.5)).then_inc(ac.h, 1)
        ac.n += 1
        rs_n = ac.n

        @sch.on("vector")
        def _(e, w=rs_n):
            e.wait_ge(ac.h, w)
            e.tensor_scalar(xnt2[:], xt2[:], lnst2[:, 2:3],
                            lnst2[:, 6:7], ALU.subtract, ALU.mult).then_inc(dv.h, 1)
        dv.n += 1
        nm_n = dv.n

        @sch.on("sync")
        def _(e, w=nm_n, i=i):
            e.wait_ge(dv.h, w)
            e.dma_start(out=xn2_d[i * 128:(i + 1) * 128, :],
                        in_=xnt2[:]).then_inc(sxw2.h, 16)
        sxw2.n += 16
    ln_done = (sxw2.n, dv.n)

    @sch.on("sync")
    def _(e, w=ln_done[0]):
        e.wait_ge(sxw2.h, w)
        for kc in range(16):
            e.dma_start(out=xnf2[:, kc * TOK2:(kc + 1) * TOK2],
                        in_=xn2_d[:, kc * 128:(kc + 1) * 128],
                        transpose=True).then_inc(sxr2.h, 16)
    sxr2.n += 256
    xnf_n = sxr2.n

    def ffn_layer(name, w_pret, nK, nM, src_sb, src_cols, dst_sb, bias_sb,
                  relu, wbufs, wsem):
        ldw = []
        ev_ns = []
        for m in range(nM):
            @sch.on("gpsimd")
            def _(e, m=m, wp=w_pret, wbufs=wbufs, nK=nK):
                if m >= 2:
                    e.wait_ge(pe.h, ldw[m - 2][1])
                e.dma_start(out=wbufs[m % 2][:, 0:nK * 128],
                            in_=wp[:, m * nK * 128:(m + 1) * nK * 128]
                            ).then_inc(wsem[m % 2].h, 16)
            wsem[m % 2].n += 16
            ldw.append([wsem[m % 2].n, None])

            @sch.on("tensor")
            def _(e, m=m, w=wsem[m % 2].n, wbufs=wbufs, nK=nK, src_sb=src_sb,
                  src_cols=src_cols, wev=(ev_ns[m - 2] if m >= 2 else 0)):
                e.wait_ge(wsem[m % 2].h, w)
                if m == 0 and src_sb is xnf2:
                    e.wait_ge(sxr2.h, xnf_n)
                if wev:
                    e.wait_ge(ac.h, wev)
                for ti in range(NKT2):
                    bank = PA[(m % 2) * NKT2 + ti]
                    for kc in range(nK):
                        e.matmul(bank[:],
                                 wbufs[m % 2][:, kc * 128:(kc + 1) * 128],
                                 src_sb[:, kc * src_cols + ti * 512:
                                        kc * src_cols + (ti + 1) * 512],
                                 start=(kc == 0), stop=(kc == nK - 1))
                e.fin(pe.h, 1)
            pe.n += 1
            ldw[m][1] = pe.n
            mm_n = pe.n

            @sch.on("scalar")
            def _(e, m=m, w=mm_n, dst_sb=dst_sb, bias_sb=bias_sb, relu=relu,
                  w2=PRE2):
                e.wait_ge(pe.h, w)
                e.wait_ge(spre2.h, w2)
                for ti in range(NKT2):
                    e.activation(dst_sb[:, m * TOK2 + ti * 512:
                                        m * TOK2 + (ti + 1) * 512],
                                 PA[(m % 2) * NKT2 + ti][:],
                                 AF.Relu if relu else AF.Copy,
                                 bias=bias_sb[:, m:m + 1]).then_inc(ac.h, 1)
            ac.n += NKT2
            ev_ns.append(ac.n)
        return ac.n

    h1_ac = ffn_layer("w1", w1_p, 16, 32, xnf2, TOK2, h1f, b1s, True,
                      wblk, swl)

    @sch.on("tensor")
    def _(e, w=h1_ac):
        e.wait_ge(ac.h, w)
        e.fin(pe.h, 1)
    pe.n += 1

    h2_ac = ffn_layer("w2", w2_p, 32, 32, h1f, TOK2, h2f, b2s, True,
                      wblk2, sw2)
    w2_mm_pe = pe.n

    @sch.on("sync")
    def _(e, w=w2_mm_pe):
        e.wait_ge(pe.h, w)
        for kc in range(32):
            j, m_ = kc // 4, kc % 4
            e.dma_start(out=oT[:, kc * TOK2:(kc + 1) * TOK2],
                        in_=cc_out[j * TOK2:(j + 1) * TOK2,
                                   m_ * 128:(m_ + 1) * 128],
                        transpose=True).then_inc(sor2.h, 16)
    sor2.n += 512
    oT_n = sor2.n

    @sch.on("tensor")
    def _(e, w=h2_ac, w2=oT_n):
        e.wait_ge(ac.h, w)
        e.wait_ge(sor2.h, w2)
        e.fin(pe.h, 1)
    pe.n += 1

    cb_ns = []
    for m in range(16):
        @sch.on("gpsimd")
        def _(e, m=m, w=(pe.n if m >= 2 else 0), w2=w2_mm_pe, w3=dv.n):
            if m == 0:
                e.wait_ge(pe.h, w2)
            if w:
                e.wait_ge(pe.h, w - 1)
            if w3 and m > 0:
                e.wait_ge(dv.h, w3)
            e.dma_start(out=wblk[m % 2][:], in_=w3_p[:, m * 4096:(m + 1) * 4096]
                        ).then_inc(sw3[m % 2].h, 16)
            e.dma_start(out=wblk2[m % 2][:], in_=wo_p[:, m * 4096:(m + 1) * 4096]
                        ).then_inc(sw3[m % 2].h, 16)
            e.dma_start(out=xTt, in_=xT2[m * 128:(m + 1) * 128, :]
                        ).then_inc(sxt2.h, 16)
        sw3[m % 2].n += 32
        sxt2.n += 16
        wl_n = sw3[m % 2].n
        xt_n = sxt2.n

        @sch.on("tensor")
        def _(e, m=m, w=wl_n, mp=m % 2, wev=(cb_ns[m - 2] if m >= 2 else 0)):
            e.wait_ge(sw3[mp].h, w)
            if wev:
                e.wait_ge(dv.h, wev)
            for ti in range(NKT2):
                bank = PA[(m % 2) * NKT2 + ti]
                for kc in range(32):
                    e.matmul(bank[:], wblk[m % 2][:, kc * 128:(kc + 1) * 128],
                             h2f[:, kc * TOK2 + ti * 512:kc * TOK2 + (ti + 1) * 512],
                             start=(kc == 0), stop=False,
                             skip_group_check=True)
                for kc in range(32):
                    e.matmul(bank[:], wblk2[m % 2][:, kc * 128:(kc + 1) * 128],
                             oT[:, kc * TOK2 + ti * 512:kc * TOK2 + (ti + 1) * 512],
                             start=False, stop=(kc == 31),
                             skip_group_check=True)
            e.fin(pe.h, 1)
        pe.n += 1
        mm_n = pe.n

        @sch.on("vector")
        def _(e, m=m, w=mm_n, w2=xt_n, w3=sot.n):
            e.wait_ge(pe.h, w)
            e.wait_ge(sxt2.h, w2)
            if w3:
                e.wait_ge(sot.h, w3)
            for ti in range(NKT2):
                e.scalar_tensor_tensor(outt[:, ti * 512:(ti + 1) * 512],
                                       PA[(m % 2) * NKT2 + ti][:], b3s[:, m:m + 1],
                                       xTt[:, ti * 512:(ti + 1) * 512],
                                       ALU.add, ALU.add)
            e.fin(dv.h, 1)
        dv.n += 1
        cb_n = dv.n
        cb_ns.append(cb_n)

        @sch.on("sync")
        def _(e, m=m, w=cb_n):
            e.wait_ge(dv.h, w)
            e.dma_start(out=outf[m * 128:(m + 1) * 128, :],
                        in_=outt).then_inc(sot.h, 16)
        sot.n += 16

    done = sot.n

    @sch.on("gpsimd")
    def _(e, w=done):
        e.wait_ge(sot.h, w)


def _l2_inputs(inputs):
    x = np.asarray(inputs["L0_attn_hidden_states"], np.float32).reshape(S, D)
    xT = np.ascontiguousarray(x.T)
    W1 = np.asarray(inputs["L0_ffn_W1"], np.float32)
    W2 = np.asarray(inputs["L0_ffn_W2"], np.float32)
    W3 = np.asarray(inputs["L0_ffn_W3"], np.float32)
    Wo = np.asarray(inputs["L0_attn_W_out"], np.float32)
    b1 = np.asarray(inputs["L0_ffn_b1"], np.float32)
    b2 = np.asarray(inputs["L0_ffn_b2"], np.float32)
    b3 = np.asarray(inputs["L0_ffn_b3"], np.float32)

    w1_p = _pretile(np.ascontiguousarray(W1.T), 16, 32)
    w2_p = _pretile(np.ascontiguousarray(W2.T), 32, 32)
    w3_p = _pretile(np.ascontiguousarray(W3.T), 32, 16)
    wo_p = _pretile(np.ascontiguousarray(Wo.T), 32, 16)
    b1_t = np.ascontiguousarray(b1.reshape(32, 128).T)
    b2_t = np.ascontiguousarray(b2.reshape(32, 128).T)
    b3_t = np.ascontiguousarray(b3.reshape(16, 128).T)

    maps = []
    for c in range(NCORE):
        t0 = c * TOK2
        maps.append(dict(
            x2=np.ascontiguousarray(x[t0:t0 + TOK2]),
            xT2=np.ascontiguousarray(xT[:, t0:t0 + TOK2]),
            w1_p=w1_p, w2_p=w2_p, w3_p=w3_p, wo_p=wo_p,
            b1_in=b1_t, b2_in=b2_t, b3_in=b3_t,
        ))
    return maps


def kernel(**inputs):
    nc = build_fused()
    m1 = _l1_inputs(inputs)
    m2 = _l2_inputs(inputs)
    maps = [dict(a, **b) for a, b in zip(m1, m2)]
    res = run_bass_kernel_spmd(nc, maps, core_ids=list(range(NCORE)))
    outs = [np.asarray(res.results[c]["outf"]) for c in range(NCORE)]
    out = np.concatenate([o.T for o in outs], axis=0)
    return out.reshape(1, S, D)


if __name__ == "__main__":
    import time, tempfile
    t0 = time.time()
    nc = build_fused()
    t1 = time.time()
    js = nc.to_json_bytes()
    from concourse.bass_utils import compile_bir_kernel
    d = tempfile.mkdtemp()
    f = compile_bir_kernel(js, d)
    t2 = time.time()
    import os
    print(f"fused build={t1-t0:.1f}s json={len(js)/1e6:.1f}MB "
          f"compile={t2-t1:.1f}s neff={os.path.getsize(f)/1e6:.1f}MB")


# version 11
# speedup vs baseline: 1.0432x; 1.0432x over previous
import numpy as np
import ml_dtypes
from contextlib import ExitStack

import concourse.bass as bass
import concourse.mybir as mybir
from concourse.bass_utils import run_bass_kernel_spmd

F32 = mybir.dt.float32
BF16 = mybir.dt.bfloat16
AF = mybir.ActivationFunctionType
ALU = mybir.AluOpType
AX = mybir.AxisListType

S = 8192
D = 2048
HV, HK, DK, DV = 32, 16, 128, 128
KEY = HK * DK
VAL = HV * DV
KS = 4
EPS = 1e-6
LN_EPS = 1e-5
C = 128
TILE = 512
NT = S // TILE
CPT = TILE // C
NCORE = 8
MASKVAL = -1e30
LNCLAMP = float(-np.log(EPS))
S_SCALE = float(-0.5 * np.log(DK))
DCLAMP = 1e30

ENGINES = ("sync", "gpsimd", "tensor", "vector", "scalar")


class Sched:
    def __init__(self):
        self.streams = {e: [] for e in ENGINES}

    def on(self, eng):
        def reg(fn):
            self.streams[eng].append(fn)
            return None
        return reg


class Ctr:
    def __init__(self, ctx, nc, name):
        self.h = ctx.enter_context(nc.semaphore(name))
        self.n = 0


_CHAIN_OPS = {
    "tensor_copy", "tensor_tensor", "tensor_add", "tensor_sub", "tensor_mul",
    "tensor_max", "tensor_scalar", "tensor_single_scalar", "tensor_scalar_add",
    "tensor_scalar_sub", "tensor_scalar_mul", "tensor_scalar_max",
    "tensor_scalar_min", "tensor_relu", "tensor_reduce", "reduce_max",
    "reduce_sum", "scalar_tensor_tensor", "activation", "memset", "memzero",
    "reciprocal", "select", "sign", "square", "sqrt", "mul", "copy", "add",
}


class InstProxy:
    def __init__(self, rec, chained, inst):
        self._rec = rec
        self._chained = chained
        self._inst = inst

    def then_inc(self, sem, k=1):
        if self._chained:
            self._rec._milestone(sem, k)
        else:
            self._inst.then_inc(sem, k)
        return self


class EngRec:
    def __init__(self, e, chain=None):
        self._e = e
        self._last = None
        self._last_ch = False
        self._chain = chain
        self._cn = 0

    def __getattr__(self, name):
        f = getattr(self._e, name)
        if callable(f):
            chain = self._chain is not None and name in _CHAIN_OPS

            def wrap(*a, **k):
                if chain and self._cn:
                    self._e.wait_ge(self._chain, self._cn)
                r = f(*a, **k)
                if chain:
                    r.then_inc(self._chain, 1)
                    self._cn += 1
                self._last = r
                self._last_ch = chain
                return InstProxy(self, chain, r)
            return wrap
        return f

    def _milestone(self, sem, k):
        self._e.wait_ge(self._chain, self._cn)
        self._e.sem_inc(sem, k)

    def fin(self, sem, k=1):
        if self._last_ch:
            self._milestone(sem, k)
        else:
            self._last.then_inc(sem, k)


def _bf(x):
    return np.ascontiguousarray(np.asarray(x).astype(ml_dtypes.bfloat16))


def _pretile(WT, nK, nM):
    t = WT.reshape(nK, 128, nM, 128).transpose(1, 2, 0, 3)
    return np.ascontiguousarray(t.reshape(128, nM * nK * 128), dtype=np.float32)



def build_fused(nt=NT):
    nc = bass.Bass("TRN2", target_bir_lowering=False)
    dt = nc.dram_tensor

    x_tok = dt("x_tok", [S, D], F32, kind="ExternalInput").ap()
    wqkv_p = dt("wqkv_p", [128, 8 * 16 * 128], F32, kind="ExternalInput").ap()
    wzab_p = dt("wzab_p", [128, 16 * 520], F32, kind="ExternalInput").ap()
    conv_in = dt("conv_in", [128, 32], F32, kind="ExternalInput").ap()
    alog_r = dt("alog_r", [1, 4], F32, kind="ExternalInput").ap()
    dtb_r = dt("dtb_r", [1, 4], F32, kind="ExternalInput").ap()
    nw_in = dt("nw_in", [128, 512], F32, kind="ExternalInput").ap()
    mS_in = dt("mS_in", [128, 128], F32, kind="ExternalInput").ap()
    mI_in = dt("mI_in", [128, 128], F32, kind="ExternalInput").ap()
    I_in = dt("I_in", [128, 128], F32, kind="ExternalInput").ap()
    U_in = dt("U_in", [128, 128], F32, kind="ExternalInput").ap()
    or_in = dt("or_in", [33, 128], F32, kind="ExternalInput").ap()
    oc_in = dt("oc_in", [128, 1], F32, kind="ExternalInput").ap()
    L_in = dt("L_in", [128, 128], F32, kind="ExternalInput").ap()

    x2 = dt("x2", [TOK2, D], F32, kind="ExternalInput").ap()
    xT2 = dt("xT2", [D, TOK2], F32, kind="ExternalInput").ap()
    w1_p = dt("w1_p", [128, 32 * 16 * 128], F32, kind="ExternalInput").ap()
    w2_p = dt("w2_p", [128, 32 * 32 * 128], F32, kind="ExternalInput").ap()
    w3_p = dt("w3_p", [128, 16 * 32 * 128], F32, kind="ExternalInput").ap()
    wo_p = dt("wo_p", [128, 16 * 32 * 128], F32, kind="ExternalInput").ap()
    b1_in = dt("b1_in", [128, 32], F32, kind="ExternalInput").ap()
    b2_in = dt("b2_in", [128, 32], F32, kind="ExternalInput").ap()
    b3_in = dt("b3_in", [128, 16], F32, kind="ExternalInput").ap()

    outf = dt("outf", [D, TOK2], F32, kind="ExternalOutput").ap()

    cc_in_h = dt("cc_in", [S, 512], BF16)
    cc_out_h = dt("cc_out", [S, 512], BF16)
    og = cc_in_h.ap()
    cc_out = cc_out_h.ap()

    xn_d = dt("xn_d", [S, D], BF16).ap()
    kv_d = dt("kv_d", [2, 768, TILE], BF16).ap()
    row_d = dt("row_d", [2, 2, 2048], F32).ap()
    xn2_d = dt("xn2_d", [TOK2, D], BF16).ap()

    sems = ExitStack()
    ctx = ExitStack()
    sb = lambda n_, sh, d_=F32: ctx.enter_context(nc.sbuf_tensor(n_, sh, d_))
    psa = lambda n_, sh, d_=F32: ctx.enter_context(nc.psum_tensor(n_, sh, d_))

    wqkv = sb("wqkv", [128, 8 * 16 * 128], BF16)
    wzab = sb("wzab", [128, 16 * 520], BF16)
    conv_w = sb("conv_w", [128, 32])
    xt = [sb(f"xt{i}", [128, D]) for i in range(2)]
    xnt = [sb(f"xnt{i}", [128, D], BF16) for i in range(2)]
    lnst = sb("lnst", [128, 8])
    xnf = sb("xnf", [128, 16 * TILE], BF16)
    qkvr = sb("qkvr", [128, 8 * (TILE + 4)], BF16)
    halo = sb("halo", [128, 8 * 4], BF16)
    convt = sb("convt", [128, TILE])
    mixed = [sb(f"mixed{i}", [128, 8 * TILE], BF16) for i in range(2)]
    ksq = sb("ksq", [128, TILE])
    vtok = [sb(f"vtok{i}", [128, CPT * 512], BF16) for i in range(2)]
    ktok = [sb(f"ktok{i}", [128, CPT * 256], BF16) for i in range(2)]
    zt = [sb(f"zt{i}", [128, CPT * 512], BF16) for i in range(2)]
    ogt = sb("ogt", [128, 512], BF16)

    gpl = sb("gpl", [128, 16])
    bpl = sb("bpl", [128, 16])
    cgc = sb("cgc", [128, 16])
    rvc = sb("rvc", [128, 16])
    cpl = sb("cpl", [128, 16])
    gam = sb("gam", [128, 16])
    lrkp = sb("lrkp", [128, 8])
    rkp = sb("rkp", [128, 8])
    lrqp = sb("lrqp", [128, 8])
    lnssq = sb("lnssq", [128, 16])
    brk = sb("brk", [128, 16])
    gban = sb("gban", [128, 16])
    gqa = sb("gqa", [128, 16])
    pla = sb("pla", [128, 16])
    plw = sb("plw", [128, 16])
    pl16 = sb("pl16", [128, 16])
    sp4 = sb("sp4", [128, 4])
    ab_sb = sb("ab_sb", [128, 32])
    nae_row = sb("nae_row", [1, 4])
    dtb_row = sb("dtb_row", [1, 4])
    nae_b = sb("nae_b", [128, 4])
    dtb_b = sb("dtb_b", [128, 4])
    rowA = sb("rowA", [16, 128])
    rowW = sb("rowW", [16, 128])
    flat = sb("flat", [33, 4096])

    exa = sb("exa", [128, 512])
    exw = sb("exw", [128, 512])
    dmt = sb("dmt", [128, 512])
    dmt2 = sb("dmt2", [128, 512])
    o1 = sb("o1", [128, 512])
    osb = sb("osb", [128, 1024])
    osq = sb("osq", [128, 512])
    rms = sb("rms", [128, 8])
    at_bf = sb("at_bf", [128, 4 * 512], BF16)
    a_bf = sb("a_bf", [128, 4 * 512], BF16)
    s2_bf = sb("s2_bf", [128, 1024], BF16)
    s3_bf = sb("s3_bf", [128, 1024], BF16)
    b_bf = sb("b_bf", [128, 512], BF16)
    x_bf = sb("x_bf", [128, 512], BF16)
    wt_bf = sb("wt_bf", [128, 4 * 512], BF16)
    dc_bf = sb("dc_bf", [128, 512], BF16)
    kh_bf = sb("kh_bf", [128, 4 * 256], BF16)
    st_f = sb("st_f", [128, 512])
    st_b = sb("st_b", [128, 512], BF16)
    idb = sb("idb", [128, 128], BF16)
    mi_sb = sb("mi_sb", [128, 128])
    ms_sb = sb("ms_sb", [128, 128])
    i_sb = sb("i_sb", [128, 128])
    u_sb = sb("u_sb", [128, 128])
    l_sb = sb("l_sb", [128, 128])
    or_sb = sb("or_sb", [33, 128])
    oc_sb = sb("oc_sb", [128, 1])
    nw_sb = sb("nw_sb", [128, 512])
    zsil4 = sb("zsil4", [128, CPT * 512], BF16)

    PJ = psa("PJ", [128, 512])
    PZ = psa("PZ", [128, 512])
    P2 = psa("P2", [128, 512])
    B3 = psa("B3", [128, 512])
    B4 = psa("B4", [128, 512])
    B5 = psa("B5", [128, 512])
    B6 = psa("B6", [128, 512])
    XB = psa("XB", [128, 512], BF16)

    spre = Ctr(sems, nc, "spre")
    sx = [Ctr(sems, nc, "sx0"), Ctr(sems, nc, "sx1")]
    sxw = [Ctr(sems, nc, "sxw0"), Ctr(sems, nc, "sxw1")]
    sxr = Ctr(sems, nc, "sxr")
    skw = Ctr(sems, nc, "skw")
    skr = [Ctr(sems, nc, "skr0"), Ctr(sems, nc, "skr1")]
    sog = Ctr(sems, nc, "sog")
    srow = Ctr(sems, nc, "srow")
    gq = Ctr(sems, nc, "gq")
    pe = Ctr(sems, nc, "pe")
    dv = Ctr(sems, nc, "dv")
    ac = Ctr(sems, nc, "ac")
    po = Ctr(sems, nc, "po")

    sch = Sched()

    @sch.on("sync")
    def _(e):
        e.dma_start(out=conv_w[:], in_=conv_in).then_inc(spre.h, 16)
        e.dma_start(out=mi_sb[:], in_=mI_in).then_inc(spre.h, 16)
        e.dma_start(out=ms_sb[:], in_=mS_in).then_inc(spre.h, 16)
        e.dma_start(out=i_sb[:], in_=I_in).then_inc(spre.h, 16)
        e.dma_start(out=u_sb[:], in_=U_in).then_inc(spre.h, 16)
        e.dma_start(out=l_sb[:], in_=L_in).then_inc(spre.h, 16)
        e.dma_start(out=or_sb[:], in_=or_in).then_inc(spre.h, 16)
        e.dma_start(out=oc_sb[:], in_=oc_in).then_inc(spre.h, 16)
        e.dma_start(out=nw_sb[:], in_=nw_in).then_inc(spre.h, 16)
        e.dma_start(out=dtb_row[:], in_=dtb_r).then_inc(spre.h, 16)
        e.dma_start(out=nae_row[:], in_=alog_r).then_inc(spre.h, 16)
    spre.n += 11 * 16
    PRE_SY = spre.n

    @sch.on("gpsimd")
    def _(e):
        e.dma_start(out=wqkv[:], in_=wqkv_p).then_inc(gq.h, 16)
        e.dma_start(out=wzab[:], in_=wzab_p).then_inc(gq.h, 16)
        e.dma_start(out=idb[:], in_=I_in).then_inc(gq.h, 16)
    gq.n += 48
    PRE_GQ = gq.n

    @sch.on("vector")
    def _(e):
        e.memset(st_f[:], 0.0)
        e.memset(st_b[:], 0.0)
        e.memset(halo[:], 0.0).then_inc(dv.h, 1)
    dv.n += 1

    @sch.on("scalar")
    def _(e, w=PRE_SY):
        e.wait_ge(spre.h, w)
        e.activation(nae_row[:], nae_row[:], AF.Exp).then_inc(ac.h, 1)
    ac.n += 1
    _w = ac.n

    @sch.on("vector")
    def _(e, w=_w):
        e.wait_ge(ac.h, w)
        e.tensor_scalar_mul(nae_row[:], nae_row[:], -1.0).then_inc(dv.h, 1)
    dv.n += 1
    _w = dv.n

    @sch.on("tensor")
    def _(e, w=_w, w2=PRE_SY):
        e.wait_ge(dv.h, w)
        e.wait_ge(spre.h, w2)
        e.matmul(P2[:, 0:4], or_sb[0:1, :], nae_row[:], start=True, stop=True)
        e.matmul(P2[:, 4:8], or_sb[0:1, :], dtb_row[:], start=False, stop=True,
                 skip_group_check=True)
        e.fin(pe.h, 1)
    pe.n += 1
    _w = pe.n

    @sch.on("vector")
    def _(e, w=_w):
        e.wait_ge(pe.h, w)
        e.tensor_copy(nae_b[:], P2[:, 0:4])
        e.tensor_copy(dtb_b[:], P2[:, 4:8]).then_inc(dv.h, 1)
    dv.n += 1
    PRE_BC = dv.n

    ln_wr = []
    xt_free = [0, 0]
    xnt_free = [0, 0]

    def emit_ln_chunk(i):
        par = i % 2
        t0 = i * 128

        @sch.on("sync")
        def _(e, t0=t0, par=par, w=xt_free[par]):
            if w:
                e.wait_ge(dv.h, w)
            e.dma_start(out=xt[par][:],
                        in_=x_tok[t0:t0 + 128, :]).then_inc(sx[par].h, 16)
        sx[par].n += 16
        ld_n = sx[par].n

        @sch.on("scalar")
        def _(e, w=ld_n, w2=xnt_free[par], par=par):
            e.wait_ge(sx[par].h, w)
            if w2:
                e.wait_ge(sxw[par].h, w2)
            e.activation(xnt[par][:], xt[par][:], AF.Square,
                         accum_out=lnst[:, 0:1]).then_inc(ac.h, 1)
        ac.n += 1
        sq_n = ac.n

        @sch.on("vector")
        def _(e, w=ld_n, w2=sq_n, par=par):
            e.wait_ge(sx[par].h, w)
            e.tensor_reduce(lnst[:, 1:2], xt[par][:], AX.X, ALU.add)
            e.wait_ge(ac.h, w2)
            e.tensor_scalar_mul(lnst[:, 2:3], lnst[:, 1:2], 1.0 / D)
            e.tensor_tensor(lnst[:, 3:4], lnst[:, 2:3], lnst[:, 2:3], ALU.mult)
            e.tensor_scalar(lnst[:, 4:5], lnst[:, 0:1], 1.0 / D, None, ALU.mult)
            e.tensor_tensor(lnst[:, 4:5], lnst[:, 4:5], lnst[:, 3:4], ALU.subtract)
            e.tensor_scalar(lnst[:, 4:5], lnst[:, 4:5], LN_EPS, None,
                            ALU.add).then_inc(dv.h, 1)
        dv.n += 1
        va_n = dv.n

        @sch.on("scalar")
        def _(e, w=va_n):
            e.wait_ge(dv.h, w)
            e.activation(lnst[:, 5:6], lnst[:, 4:5], AF.Ln)
            e.activation(lnst[:, 6:7], lnst[:, 5:6], AF.Exp,
                         scale=np.float32(-0.5)).then_inc(ac.h, 1)
        ac.n += 1
        rs_n = ac.n

        @sch.on("vector")
        def _(e, w=rs_n, par=par):
            e.wait_ge(ac.h, w)
            e.tensor_scalar(xnt[par][:], xt[par][:], lnst[:, 2:3],
                            lnst[:, 6:7], ALU.subtract, ALU.mult).then_inc(dv.h, 1)
        dv.n += 1
        xt_free[par] = dv.n
        nm_n = dv.n

        @sch.on("sync")
        def _(e, w=nm_n, t0=t0, par=par):
            e.wait_ge(dv.h, w)
            e.dma_start(out=xn_d[t0:t0 + 128, :],
                        in_=xnt[par][:]).then_inc(sxw[par].h, 16)
        sxw[par].n += 16
        xnt_free[par] = sxw[par].n
        ln_wr.append((sxw[0].n, sxw[1].n))

    for i in range(min(12, 4 * nt)):
        emit_ln_chunk(i)

    rel_pe = [0] * (NT + 4)
    rel_po = [0] * (NT + 4)
    rel_dv = [0] * (NT + 4)
    prev_ab_pe = 0
    prev_scn = 0
    prev_o1 = 0
    prev_tile_atx = 0
    prev_sol_ac = 0

    for t in range(nt):
        par = t % 2
        need_ln = ln_wr[4 * t + 3]

        @sch.on("sync")
        def _(e, t=t, par=par, w=need_ln, w2=prev_ab_pe):
            if w[0]:
                e.wait_ge(sxw[0].h, w[0])
            if w[1]:
                e.wait_ge(sxw[1].h, w[1])
            if w2:
                e.wait_ge(pe.h, w2)
            for kc in range(16):
                e.dma_start(out=xnf[:, kc * TILE:(kc + 1) * TILE],
                            in_=xn_d[t * TILE:(t + 1) * TILE,
                                     kc * 128:(kc + 1) * 128],
                            transpose=True).then_inc(sxr.h, 16)
        sxr.n += 256
        xnf_n = sxr.n

        evacs_q = [0, 0]
        for g in range(8):
            prev_evac = evacs_q[g % 2]
            @sch.on("tensor")
            def _(e, t=t, par=par, g=g, w=xnf_n, w2=prev_evac, w3=rel_po[t],
                  w4=prev_scn, w5=prev_sol_ac):
                if g == 0:
                    e.wait_ge(sxr.h, w)
                    if t == 0:
                        e.wait_ge(gq.h, PRE_GQ)
                    if w3:
                        e.wait_ge(po.h, w3)
                    if w4:
                        e.wait_ge(dv.h, w4)
                    if w5:
                        e.wait_ge(ac.h, w5)
                else:
                    if w2:
                        e.wait_ge(dv.h, w2)
                    if g == 1 and w4:
                        e.wait_ge(dv.h, w4)
                bank = PJ if g % 2 == 0 else B3
                for kc in range(16):
                    e.matmul(bank[:], wqkv[:, g * 2048 + kc * 128:
                                           g * 2048 + (kc + 1) * 128],
                             xnf[:, kc * TILE:(kc + 1) * TILE],
                             start=(kc == 0), stop=(kc == 15))
                e.fin(pe.h, 1)
            pe.n += 1
            mm_n = pe.n

            @sch.on("vector")
            def _(e, g=g, w=mm_n, w2=rel_po[t]):
                base = g * (TILE + 4)
                e.wait_ge(pe.h, w)
                if w2:
                    e.wait_ge(po.h, w2)
                e.tensor_copy(qkvr[:, base:base + 4], halo[:, g * 4:(g + 1) * 4])
                e.tensor_copy(qkvr[:, base + 4:base + 4 + TILE],
                              (PJ if g % 2 == 0 else B3)[:])
                e.tensor_copy(halo[:, g * 4:(g + 1) * 4],
                              qkvr[:, base + TILE:base + TILE + 4]).then_inc(dv.h, 1)
            dv.n += 1
            evacs_q[g % 2] = dv.n
            evac_g = dv.n

            @sch.on("vector")
            def _(e, g=g, w=evac_g, t=t, w2=rel_pe[t], w3=ac.n):
                base = g * (TILE + 4)
                e.wait_ge(dv.h, w)
                e.wait_ge(ac.h, w3)
                if t == 0 and g == 0:
                    e.wait_ge(spre.h, PRE_SY)
                if w2 and g == 0:
                    e.wait_ge(pe.h, w2)
                e.tensor_scalar(convt[:], qkvr[:, base + 1:base + 1 + TILE],
                                conv_w[:, g * 4:g * 4 + 1], None, ALU.mult)
                for k in range(1, 4):
                    e.scalar_tensor_tensor(convt[:],
                                           qkvr[:, base + 1 + k:base + 1 + k + TILE],
                                           conv_w[:, g * 4 + k:g * 4 + k + 1],
                                           convt[:], ALU.mult, ALU.add)
                e.fin(dv.h, 1)
            dv.n += 1
            cv_n = dv.n

            @sch.on("scalar")
            def _(e, g=g, w=cv_n, par=par):
                e.wait_ge(dv.h, w)
                e.activation(mixed[par][:, g * TILE:(g + 1) * TILE],
                             convt[:], AF.Silu).then_inc(ac.h, 1)
            ac.n += 1

            if g >= 2:
                @sch.on("sync")
                def _(e, par=par, g=g, w=ac.n, w2=rel_dv[t]):
                    e.wait_ge(ac.h, w)
                    if g == 2 and w2:
                        e.wait_ge(dv.h, w2)
                    gi = g - 2
                    e.dma_start(out=kv_d[par, gi * 128:(gi + 1) * 128, :],
                                in_=mixed[par][:, g * TILE:(g + 1) * TILE]
                                ).then_inc(skw.h, 16)
                skw.n += 16
                wr_n = skw.n

                if g == 3:
                    @sch.on("sync")
                    def _(e, par=par, w=wr_n):
                        e.wait_ge(skw.h, w)
                        for c in range(CPT):
                            e.dma_start(out=ktok[par][:, c * 256:(c + 1) * 256],
                                        in_=kv_d[par, 0:256,
                                                 c * 128:(c + 1) * 128],
                                        transpose=True).then_inc(skr[par].h, 16)
                    skr[par].n += 4 * 16
                elif g >= 4:
                    hh = g - 4

                    @sch.on("sync")
                    def _(e, par=par, hh=hh, w=wr_n):
                        e.wait_ge(skw.h, w)
                        for c in range(CPT):
                            e.dma_start(
                                out=vtok[par][:, c * 512 + hh * 128:
                                              c * 512 + (hh + 1) * 128],
                                in_=kv_d[par, 256 + hh * 128:
                                         256 + (hh + 1) * 128,
                                         c * 128:(c + 1) * 128],
                                transpose=True).then_inc(skr[par].h, 16)
                    skr[par].n += 4 * 16
        mixed_ac = ac.n
        kvr_n = skr[par].n

        prev_z = 0
        for c in range(CPT):
            @sch.on("tensor")
            def _(e, par=par, c=c, w=prev_z, w2=rel_po[t], w4=prev_scn,
                  w5=prev_sol_ac):
                if w:
                    e.wait_ge(dv.h, w)
                if c == 0 and w2:
                    e.wait_ge(po.h, w2)
                if c == 0 and w4:
                    e.wait_ge(dv.h, w4)
                if c == 0 and w5:
                    e.wait_ge(ac.h, w5)
                for kc in range(16):
                    lhs = xnf[:, kc * TILE + c * 128: kc * TILE + (c + 1) * 128]
                    e.matmul(PZ[:], lhs, wzab[:, kc * 520:kc * 520 + 512],
                             start=(kc == 0), stop=(kc == 15))
                e.fin(pe.h, 1)
            pe.n += 1
            zmm_n = pe.n

            @sch.on("vector")
            def _(e, c=c, par=par, w=zmm_n, w2=rel_po[t]):
                e.wait_ge(pe.h, w)
                if c == 0 and w2:
                    e.wait_ge(po.h, w2)
                e.tensor_copy(zt[par][:, c * 512:(c + 1) * 512],
                              PZ[:]).then_inc(dv.h, 1)
            dv.n += 1
            prev_z = dv.n
        z_dv = dv.n

        @sch.on("scalar")
        def _(e, par=par, w=z_dv, w2=po.n):
            e.wait_ge(dv.h, w)
            if w2:
                e.wait_ge(po.h, w2)
            for c in range(CPT):
                e.activation(zsil4[:, c * 512:(c + 1) * 512],
                             zt[par][:, c * 512:(c + 1) * 512],
                             AF.Silu).then_inc(ac.h, 1)
        ac.n += CPT
        zsil_ac = ac.n

        @sch.on("tensor")
        def _(e, par=par, w=z_dv):
            e.wait_ge(dv.h, w)
            for c in range(CPT):
                for kc in range(16):
                    lhs = xnf[:, kc * TILE + c * 128: kc * TILE + (c + 1) * 128]
                    e.matmul(PZ[:, c * 8:(c + 1) * 8], lhs,
                             wzab[:, kc * 520 + 512:(kc + 1) * 520],
                             start=(kc == 0 and c == 0), stop=(kc == 15),
                             skip_group_check=True)
            e.fin(pe.h, 1)
        pe.n += 1
        ab_pe = pe.n
        prev_ab_pe = ab_pe

        @sch.on("vector")
        def _(e, w=ab_pe):
            e.wait_ge(pe.h, w)
            e.tensor_copy(ab_sb[:], PZ[:, 0:32]).then_inc(dv.h, 1)
        dv.n += 1
        ab_dv = dv.n

        for i in range(4 * (t + 3), min(4 * (t + 4), 4 * nt)):
            emit_ln_chunk(i)

        @sch.on("scalar")
        def _(e, w=ab_dv, w2=PRE_BC, w3=zsil_ac):
            e.wait_ge(dv.h, max(w, w2))
            e.wait_ge(ac.h, w3)
            for c in range(CPT):
                e.activation(pl16[:, c * 4:(c + 1) * 4],
                             ab_sb[:, c * 8 + 4:c * 8 + 8], AF.Abs)
            e.fin(ac.h, 1)
        ac.n += 1
        babs_ac = ac.n

        @sch.on("scalar")
        def _(e, w=babs_ac):
            e.activation(pl16[:], pl16[:], AF.Exp, scale=np.float32(-1.0))
            e.activation(pl16[:], pl16[:], AF.Ln,
                         bias=np.float32(1.0)).then_inc(ac.h, 1)
        ac.n += 1
        bl_ac = ac.n

        @sch.on("vector")
        def _(e, w=bl_ac):
            e.wait_ge(ac.h, w)
            for c in range(CPT):
                e.tensor_scalar(bpl[:, c * 4:(c + 1) * 4],
                                ab_sb[:, c * 8 + 4:c * 8 + 8], -1.0, 0.0,
                                ALU.mult, ALU.max)
            e.tensor_tensor(bpl[:], bpl[:], pl16[:], ALU.add).then_inc(dv.h, 1)
        dv.n += 1
        bsp_dv = dv.n

        @sch.on("scalar")
        def _(e, w=bsp_dv):
            e.wait_ge(dv.h, w)
            e.activation(bpl[:], bpl[:], AF.Exp,
                         scale=np.float32(-1.0)).then_inc(ac.h, 1)
        ac.n += 1
        bpl_dv = dv.n
        bpl_ac = ac.n

        for c in range(CPT):
            @sch.on("vector")
            def _(e, c=c, w=ab_dv, w2=bpl_ac):
                e.wait_ge(dv.h, w)
                e.wait_ge(ac.h, w2)
                e.tensor_tensor(sp4[:], ab_sb[:, c * 8:c * 8 + 4], dtb_b[:],
                                ALU.add).then_inc(dv.h, 1)
            dv.n += 1
            t1_dv = dv.n

            @sch.on("scalar")
            def _(e, c=c, w=t1_dv):
                e.wait_ge(dv.h, w)
                e.activation(pl16[:, c * 4:(c + 1) * 4], sp4[:], AF.Abs)
                e.activation(pl16[:, c * 4:(c + 1) * 4],
                             pl16[:, c * 4:(c + 1) * 4], AF.Exp,
                             scale=np.float32(-1.0))
                e.activation(pl16[:, c * 4:(c + 1) * 4],
                             pl16[:, c * 4:(c + 1) * 4], AF.Ln,
                             bias=np.float32(1.0)).then_inc(ac.h, 1)
            ac.n += 1
            l1p_ac = ac.n

            @sch.on("vector")
            def _(e, c=c, w=l1p_ac):
                e.wait_ge(ac.h, w)
                e.tensor_scalar_max(sp4[:], sp4[:], 0.0)
                e.tensor_tensor(sp4[:], sp4[:], pl16[:, c * 4:(c + 1) * 4], ALU.add)
                e.tensor_tensor(gpl[:, c * 4:(c + 1) * 4], sp4[:], nae_b[:],
                                ALU.mult).then_inc(dv.h, 1)
            dv.n += 1
        gpl_dv = dv.n

        @sch.on("scalar")
        def _(e, par=par):
            e.activation(ksq[:], mixed[par][:, 2 * TILE:3 * TILE],
                         AF.Square).then_inc(ac.h, 1)
        ac.n += 1
        k0_ac = ac.n

        @sch.on("tensor")
        def _(e, w=gpl_dv, w2=k0_ac):
            e.wait_ge(dv.h, w)
            e.wait_ge(ac.h, w2)
            e.matmul(P2[:, 0:16], u_sb[:], gpl[:], start=True, stop=True)
            e.matmul(P2[:, 16:32], l_sb[:], gpl[:], start=False, stop=True,
                     skip_group_check=True)
            for c in range(CPT):
                e.matmul(P2[:, 32 + c * 2:33 + c * 2],
                         ksq[:, c * 128:(c + 1) * 128], oc_sb[:],
                         start=False, stop=True, skip_group_check=True)
            e.fin(pe.h, 1)
        pe.n += 1
        n0_pe = pe.n

        @sch.on("scalar")
        def _(e, par=par, w=n0_pe):
            e.wait_ge(pe.h, w)
            e.activation(ksq[:], mixed[par][:, 3 * TILE:4 * TILE],
                         AF.Square).then_inc(ac.h, 1)
        ac.n += 1
        k1_ac = ac.n

        @sch.on("tensor")
        def _(e, w=k1_ac):
            e.wait_ge(ac.h, w)
            for c in range(CPT):
                e.matmul(P2[:, 33 + c * 2:34 + c * 2],
                         ksq[:, c * 128:(c + 1) * 128], oc_sb[:],
                         start=False, stop=True, skip_group_check=True)
            e.fin(pe.h, 1)
        pe.n += 1
        n1_pe = pe.n

        @sch.on("scalar")
        def _(e, par=par, w=n1_pe):
            e.wait_ge(pe.h, w)
            e.activation(ksq[:], mixed[par][:, 0:TILE],
                         AF.Square).then_inc(ac.h, 1)
        ac.n += 1
        q0_ac = ac.n

        @sch.on("tensor")
        def _(e, w=q0_ac):
            e.wait_ge(ac.h, w)
            for c in range(CPT):
                e.matmul(P2[:, 40 + c * 2:41 + c * 2],
                         ksq[:, c * 128:(c + 1) * 128], oc_sb[:],
                         start=False, stop=True, skip_group_check=True)
            e.fin(pe.h, 1)
        pe.n += 1
        n2_pe = pe.n

        @sch.on("scalar")
        def _(e, par=par, w=n2_pe):
            e.wait_ge(pe.h, w)
            e.activation(ksq[:], mixed[par][:, TILE:2 * TILE],
                         AF.Square).then_inc(ac.h, 1)
        ac.n += 1
        q1_ac = ac.n

        @sch.on("tensor")
        def _(e, w=q1_ac):
            e.wait_ge(ac.h, w)
            for c in range(CPT):
                e.matmul(P2[:, 41 + c * 2:42 + c * 2],
                         ksq[:, c * 128:(c + 1) * 128], oc_sb[:],
                         start=False, stop=True, skip_group_check=True)
            e.fin(pe.h, 1)
        pe.n += 1
        pln_pe = pe.n

        @sch.on("vector")
        def _(e, w=pln_pe):
            e.wait_ge(pe.h, w)
            e.tensor_copy(cgc[:], P2[:, 0:16])
            e.tensor_copy(rvc[:], P2[:, 16:32]).then_inc(dv.h, 1)
        dv.n += 1
        cg_dv = dv.n

        @sch.on("scalar")
        def _(e, w=pln_pe):
            e.wait_ge(pe.h, w)
            e.activation(lnssq[:], P2[:, 32:48], AF.Ln).then_inc(ac.h, 1)
        ac.n += 1
        lnq_ac = ac.n

        @sch.on("vector")
        def _(e, w=lnq_ac):
            e.wait_ge(ac.h, w)
            e.tensor_scalar(lrkp[:], lnssq[:, 0:8], -0.5, LNCLAMP,
                            ALU.mult, ALU.min)
            e.tensor_scalar(lrqp[:], lnssq[:, 8:16], -0.5, LNCLAMP,
                            ALU.mult, ALU.min)
            e.tensor_scalar(lrqp[:], lrqp[:], S_SCALE, None, ALU.add).then_inc(dv.h, 1)
        dv.n += 1
        lr_dv = dv.n

        @sch.on("scalar")
        def _(e, w=lr_dv):
            e.wait_ge(dv.h, w)
            e.activation(rkp[:], lrkp[:], AF.Exp).then_inc(ac.h, 1)
        ac.n += 1
        rk_ac = ac.n

        @sch.on("vector")
        def _(e, w=rk_ac, w2=cg_dv):
            e.wait_ge(ac.h, w)
            e.wait_ge(dv.h, w2)
            lrk4 = lrkp[:].rearrange("p (c j) -> p c j", c=4)
            lrq4 = lrqp[:].rearrange("p (c j) -> p c j", c=4)
            rk4 = rkp[:].rearrange("p (c j) -> p c j", c=4)
            pla4 = pla[:].rearrange("p (c h) -> p c h", c=4)
            plw4 = plw[:].rearrange("p (c h) -> p c h", c=4)
            pl4 = pl16[:].rearrange("p (c h) -> p c h", c=4)
            for r in range(2):
                e.tensor_copy(pla4[:, :, r::2], lrk4)
                e.tensor_copy(plw4[:, :, r::2], lrq4)
                e.tensor_copy(pl4[:, :, r::2], rk4)
            e.tensor_tensor(brk[:], bpl[:], pl16[:], ALU.mult)
            e.tensor_tensor(pla[:], pla[:], cgc[:], ALU.add)
            e.tensor_tensor(plw[:], plw[:], cgc[:], ALU.add).then_inc(dv.h, 1)
        dv.n += 1
        plm_dv = dv.n

        @sch.on("scalar")
        def _(e, w=plm_dv):
            e.wait_ge(dv.h, w)
            e.activation(gban[:], pla[:], AF.Exp)
            e.activation(gqa[:], plw[:], AF.Exp)
            e.activation(cpl[:], rvc[:], AF.Exp).then_inc(ac.h, 1)
        ac.n += 1
        gsc_ac = ac.n

        @sch.on("vector")
        def _(e, w=gsc_ac):
            e.wait_ge(ac.h, w)
            e.tensor_scalar_mul(gban[:], gban[:], -1.0)
            e.tensor_tensor(pl16[:], cgc[:], rvc[:], ALU.add)
            e.tensor_tensor(cpl[:], cpl[:], bpl[:], ALU.mult).then_inc(dv.h, 1)
        dv.n += 1
        gam_dv = dv.n

        @sch.on("scalar")
        def _(e, w=gam_dv):
            e.wait_ge(dv.h, w)
            e.activation(gam[:], pl16[:], AF.Exp).then_inc(ac.h, 1)
        ac.n += 1
        gam_ac = ac.n

        @sch.on("tensor")
        def _(e, w=plm_dv):
            e.wait_ge(dv.h, w)
            e.matmul(B5[0:16, 0:128], pla[:], i_sb[:], start=True, stop=True)
            e.matmul(B5[0:16, 128:256], plw[:], i_sb[:], start=False, stop=True,
                     skip_group_check=True)
            e.fin(pe.h, 1)
        pe.n += 1
        rows_pe = pe.n

        @sch.on("vector")
        def _(e, w=rows_pe):
            e.wait_ge(pe.h, w)
            e.tensor_copy(rowA[:], B5[0:16, 0:128])
            e.tensor_copy(rowW[:], B5[0:16, 128:256]).then_inc(dv.h, 1)
        dv.n += 1
        rows_dv = dv.n

        @sch.on("gpsimd")
        def _(e, par=par, w=rows_dv):
            e.wait_ge(dv.h, w)
            e.dma_start(out=row_d[par, 0].rearrange("(a b) -> a b", a=16),
                        in_=rowA[:]).then_inc(srow.h, 16)
            e.dma_start(out=row_d[par, 1].rearrange("(a b) -> a b", a=16),
                        in_=rowW[:]).then_inc(srow.h, 16)
        srow.n += 32
        roww_n = srow.n

        @sch.on("gpsimd")
        def _(e, par=par, w=roww_n):
            e.wait_ge(srow.h, w)
            e.dma_start(out=flat[par * 32:par * 32 + 1, :],
                        in_=row_d[par].rearrange("a b -> (a b)"
                                                 ).rearrange("(a b) -> a b", a=1)
                        ).then_inc(srow.h, 16)
        srow.n += 16
        flat_n = srow.n

        atp = at_bf
        prep_awb = []
        prep_atx = []
        prev_dbc = 0
        for c in range(CPT):
            @sch.on("tensor")
            def _(e, par=par, c=c, w=flat_n, w2=gam_ac, w3=kvr_n,
                  w4=(prep_awb[c - 1][1] if c else dv.n)):
                e.wait_ge(srow.h, w)
                if c == 0:
                    e.wait_ge(ac.h, w2)
                    e.wait_ge(skr[par].h, w3)
                    e.wait_ge(dv.h, w4)
                else:
                    e.wait_ge(dv.h, w4)
                for j in range(2):
                    kf = mixed[par][:, (2 + j) * TILE + c * 128:
                                    (2 + j) * TILE + (c + 1) * 128]
                    qf = mixed[par][:, j * TILE + c * 128:
                                    j * TILE + (c + 1) * 128]
                    e.matmul(B3[:, j * 256:j * 256 + 128], kf, kf,
                             start=(j == 0), stop=True, skip_group_check=True)
                    e.matmul(B3[:, j * 256 + 128:j * 256 + 256], kf, qf,
                             start=False, stop=True, skip_group_check=True)
                e.matmul(B5[:], or_sb[par * 32:par * 32 + 1, :],
                         flat[par * 32:par * 32 + 1, c * 512:(c + 1) * 512],
                         start=True, stop=True, skip_group_check=True)
                e.matmul(B6[:], or_sb[par * 32:par * 32 + 1, :],
                         flat[par * 32:par * 32 + 1,
                              2048 + c * 512:2048 + (c + 1) * 512],
                         start=True, stop=True, skip_group_check=True)
                e.fin(pe.h, 1)
            pe.n += 1
            pp_pe = pe.n

            @sch.on("vector")
            def _(e, c=c, w=pp_pe):
                e.wait_ge(pe.h, w)
                for h in range(4):
                    e.scalar_tensor_tensor(
                        dmt[:, h * 128:(h + 1) * 128],
                        B5[:, h * 128:(h + 1) * 128],
                        cgc[:, c * 4 + h:c * 4 + h + 1],
                        ms_sb[:], ALU.subtract, ALU.add)
                for h in range(4):
                    e.scalar_tensor_tensor(
                        dmt2[:, h * 128:(h + 1) * 128],
                        B6[:, h * 128:(h + 1) * 128],
                        cgc[:, c * 4 + h:c * 4 + h + 1],
                        mi_sb[:], ALU.subtract, ALU.add)
                e.fin(dv.h, 1)
            dv.n += 1
            dm_dv = dv.n

            @sch.on("scalar")
            def _(e, w=dm_dv):
                e.wait_ge(dv.h, w)
                e.activation(exa[:], dmt[:], AF.Exp)
                e.activation(exw[:], dmt2[:], AF.Exp).then_inc(ac.h, 1)
            ac.n += 1
            ex_ac = ac.n

            @sch.on("vector")
            def _(e, c=c, par=par, w=pp_pe, w2=ex_ac):
                e.wait_ge(pe.h, w)
                e.wait_ge(ac.h, w2)
                for h in range(4):
                    j = h // 2
                    e.scalar_tensor_tensor(
                        at_bf[:, c * 512 + h * 128:c * 512 + (h + 1) * 128],
                        B3[:, j * 256:j * 256 + 128],
                        brk[:, c * 4 + h:c * 4 + h + 1],
                        exa[:, h * 128:(h + 1) * 128], ALU.mult, ALU.mult)
                    e.scalar_tensor_tensor(
                        wt_bf[:, c * 512 + h * 128:c * 512 + (h + 1) * 128],
                        B3[:, j * 256 + 128:j * 256 + 256],
                        brk[:, c * 4 + h:c * 4 + h + 1],
                        exw[:, h * 128:(h + 1) * 128], ALU.mult, ALU.mult)
                e.fin(dv.h, 1)
            dv.n += 1
            prep_awb.append((None, dv.n))
            aw_dv = dv.n

            @sch.on("tensor")
            def _(e, c=c, w=aw_dv, w2=(prep_atx[c - 1] if c else prev_tile_atx)):
                e.wait_ge(dv.h, w)
                if w2:
                    e.wait_ge(ac.h, w2)
                for h in range(4):
                    e.transpose(XB[:, h * 128:(h + 1) * 128],
                                at_bf[:, c * 512 + h * 128:
                                      c * 512 + (h + 1) * 128], idb[:])
                e.fin(pe.h, 1)
            pe.n += 1
            tr_pe = pe.n

            @sch.on("scalar")
            def _(e, c=c, w=tr_pe):
                e.wait_ge(pe.h, w)
                e.copy(a_bf[:, c * 512:(c + 1) * 512], XB[:]).then_inc(ac.h, 1)
            ac.n += 1
            prep_atx.append(ac.n)
        @sch.on("vector")
        def _(e, par=par, w=kvr_n, w2=rk_ac, w3=prev_scn):
            e.wait_ge(skr[par].h, w)
            e.wait_ge(ac.h, w2)
            if w3:
                e.wait_ge(dv.h, w3)
            for c2 in range(CPT):
                for j in range(2):
                    e.tensor_scalar(kh_bf[:, c2 * 256 + j * 128:
                                          c2 * 256 + (j + 1) * 128],
                                    ktok[par][:, c2 * 256 + j * 128:
                                              c2 * 256 + (j + 1) * 128],
                                    rkp[:, c2 * 2 + j:c2 * 2 + j + 1], None,
                                    ALU.mult)
            e.fin(dv.h, 1)
        dv.n += 1

        prep_done_dv = dv.n
        prep_done_ac = ac.n
        prev_tile_atx = prep_atx[-1]

        for c in range(CPT):
            @sch.on("tensor")
            def _(e, par=par, c=c, w=prep_done_dv, w2=prev_scn, w3=prev_o1):
                if c == 0:
                    e.wait_ge(dv.h, w)
                if w2:
                    e.wait_ge(dv.h, max(w2, w3))
                for j in range(2):
                    kf = mixed[par][:, (2 + j) * TILE + c * 128:
                                    (2 + j) * TILE + (c + 1) * 128]
                    qf = mixed[par][:, j * TILE + c * 128:
                                    j * TILE + (c + 1) * 128]
                    e.matmul(B4[:, j * 256:(j + 1) * 256], kf,
                             st_b[:, j * 256:(j + 1) * 256],
                             start=(j == 0), stop=True, skip_group_check=True)
                    e.matmul(P2[:, j * 256:(j + 1) * 256], qf,
                             st_b[:, j * 256:(j + 1) * 256],
                             start=(j == 0), stop=True, skip_group_check=True)
                e.fin(pe.h, 1)
            pe.n += 1
            bld_pe = pe.n

            @sch.on("vector")
            def _(e, c=c, par=par, w=bld_pe):
                e.wait_ge(pe.h, w)
                for h in range(4):
                    e.scalar_tensor_tensor(
                        b_bf[:, h * 128:(h + 1) * 128],
                        B4[:, (h // 2) * 256 + (h % 2) * 128:
                           (h // 2) * 256 + (h % 2 + 1) * 128],
                        gban[:, c * 4 + h:c * 4 + h + 1],
                        vtok[par][:, c * 512 + h * 128:c * 512 + (h + 1) * 128],
                        ALU.mult, ALU.add)
                e.fin(dv.h, 1)
            dv.n += 1
            b_dv = dv.n

            @sch.on("tensor")
            def _(e, c=c, w=b_dv):
                e.wait_ge(dv.h, w)
                for h in range(4):
                    e.matmul(PZ[:, h * 128:(h + 1) * 128],
                             at_bf[:, c * 512 + h * 128:c * 512 + (h + 1) * 128],
                             b_bf[:, h * 128:(h + 1) * 128],
                             start=(h == 0), stop=True, skip_group_check=True)
                e.fin(pe.h, 1)
            pe.n += 1
            ff_pe = pe.n

            @sch.on("vector")
            def _(e, w=ff_pe):
                e.wait_ge(pe.h, w)
                e.tensor_tensor(x_bf[:], b_bf[:], PZ[:], ALU.subtract)
                e.fin(dv.h, 1)
            dv.n += 1
            x_dv = dv.n

            xs = [x_bf, b_bf]
            for lvl in range(1, 7):
                if lvl == 1:
                    at_s = lambda h, c=c: at_bf[:, c * 512 + h * 128:
                                                c * 512 + (h + 1) * 128]
                    a_s = lambda h, c=c: a_bf[:, c * 512 + h * 128:
                                              c * 512 + (h + 1) * 128]
                else:
                    ssrc = s2_bf if (lvl - 1) % 2 == 1 else s3_bf
                    at_s = lambda h, ssrc=ssrc: ssrc[:, h * 256:h * 256 + 128]
                    a_s = lambda h, ssrc=ssrc: ssrc[:, h * 256 + 128:h * 256 + 256]
                sdst = s2_bf if lvl % 2 == 1 else s3_bf
                at_d = lambda h, sdst=sdst: sdst[:, h * 256:h * 256 + 128]
                x_s = xs[(lvl - 1) % 2]
                x_d = xs[lvl % 2]

                @sch.on("tensor")
                def _(e, at_s=at_s, a_s=a_s, lvl=lvl, w=dv.n, w2=ac.n):
                    e.wait_ge(dv.h, w)
                    e.wait_ge(ac.h, w2)
                    for h in range(4):
                        o = (h % 2) * 256
                        bank = PZ if h < 2 else PJ
                        e.matmul(bank[:, o:o + 128], a_s(h), at_s(h),
                                 start=(h % 2 == 0), stop=True,
                                 skip_group_check=True)
                        if lvl < 6:
                            e.matmul(bank[:, o + 128:o + 256], at_s(h), a_s(h),
                                     start=False, stop=True,
                                     skip_group_check=True)
                    e.fin(pe.h, 1)
                pe.n += 1
                sq_pe = pe.n

                @sch.on("scalar")
                def _(e, sdst=sdst, lvl=lvl, w=sq_pe):
                    e.wait_ge(pe.h, w)
                    e.copy(sdst[:, 0:512], PZ[:])
                    e.copy(sdst[:, 512:1024], PJ[:])
                    e.fin(ac.h, 1)
                ac.n += 1
                sqe_ac = ac.n

                @sch.on("tensor")
                def _(e, at_d=at_d, x_s=x_s, w=sqe_ac):
                    e.wait_ge(ac.h, w)
                    for h in range(4):
                        o = (h % 2) * 128
                        bank = PZ if h < 2 else PJ
                        e.matmul(bank[:, o:o + 128], at_d(h),
                                 x_s[:, h * 128:(h + 1) * 128],
                                 start=(h % 2 == 0), stop=False,
                                 skip_group_check=True)
                        e.matmul(bank[:, o:o + 128], idb[:],
                                 x_s[:, h * 128:(h + 1) * 128],
                                 start=False, stop=True,
                                 skip_group_check=True)
                    e.fin(pe.h, 1)
                pe.n += 1
                ap_pe = pe.n

                @sch.on("scalar")
                def _(e, x_d=x_d, w=ap_pe):
                    e.wait_ge(pe.h, w)
                    e.copy(x_d[:, 0:256], PZ[:, 0:256])
                    e.copy(x_d[:, 256:512], PJ[:, 0:256])
                    e.fin(ac.h, 1)
                ac.n += 1
            sol_ac = ac.n
            prev_sol_ac = sol_ac
            XFIN = xs[0]

            @sch.on("vector")
            def _(e, c=c, par=par, w=sol_ac):
                e.wait_ge(ac.h, w)
                for h in range(4):
                    j = h // 2
                    e.tensor_scalar(o1[:, h * 128:(h + 1) * 128],
                                    P2[:, j * 256 + (h % 2) * 128:
                                       j * 256 + (h % 2 + 1) * 128],
                                    gqa[:, c * 4 + h:c * 4 + h + 1], None,
                                    ALU.mult)
                    e.tensor_scalar(dc_bf[:, h * 128:(h + 1) * 128],
                                    XFIN[:, h * 128:(h + 1) * 128],
                                    cpl[:, c * 4 + h:c * 4 + h + 1], None,
                                    ALU.mult)
                e.fin(dv.h, 1)
            dv.n += 1
            odk_dv = dv.n

            @sch.on("tensor")
            def _(e, c=c, w=odk_dv):
                e.wait_ge(dv.h, w)
                for h in range(4):
                    bank = PZ if h < 2 else PJ
                    o = (h % 2) * 256
                    e.matmul(bank[:, o:o + 128],
                             wt_bf[:, c * 512 + h * 128:c * 512 + (h + 1) * 128],
                             XFIN[:, h * 128:(h + 1) * 128],
                             start=(h % 2 == 0), stop=True,
                             skip_group_check=True)
                    e.matmul(bank[:, o + 128:o + 256],
                             kh_bf[:, c * 256 + (h // 2) * 128:
                                   c * 256 + (h // 2 + 1) * 128],
                             dc_bf[:, h * 128:(h + 1) * 128],
                             start=False, stop=True, skip_group_check=True)
                e.fin(pe.h, 1)
            pe.n += 1
            ws_pe = pe.n

            osb_sl = osb[:, (c % 2) * 512:(c % 2 + 1) * 512]

            @sch.on("vector")
            def _(e, c=c, w=ws_pe, osl=osb_sl, w2=rel_po[t] if False else po.n):
                e.wait_ge(pe.h, w)
                if w2:
                    e.wait_ge(po.h, w2)
                for h in range(4):
                    bank = PZ if h < 2 else PJ
                    o = (h % 2) * 256
                    e.tensor_tensor(osl[:, h * 128:(h + 1) * 128],
                                    o1[:, h * 128:(h + 1) * 128],
                                    bank[:, o:o + 128], ALU.add)
                    e.scalar_tensor_tensor(st_f[:, h * 128:(h + 1) * 128],
                                           st_f[:, h * 128:(h + 1) * 128],
                                           gam[:, c * 4 + h:c * 4 + h + 1],
                                           bank[:, o + 128:o + 256],
                                           ALU.mult, ALU.add)
                    e.tensor_copy(st_b[:, h * 128:(h + 1) * 128],
                                  st_f[:, h * 128:(h + 1) * 128])
                e.fin(dv.h, 1)
            dv.n += 1
            scn_dv = dv.n
            prev_scn = scn_dv
            prev_o1 = scn_dv

            @sch.on("scalar")
            def _(e, w=scn_dv, osl=osb_sl):
                e.wait_ge(dv.h, w)
                e.activation(osq[:], osl, AF.Square).then_inc(ac.h, 1)
            ac.n += 1
            osq_ac = ac.n

            @sch.on("vector")
            def _(e, w=osq_ac):
                e.wait_ge(ac.h, w)
                e.tensor_reduce(rms[:, 0:4].rearrange("p (g o) -> p g o", o=1),
                                osq[:].rearrange("p (g d) -> p g d", g=4),
                                AX.X, ALU.add)
                e.fin(dv.h, 1)
            dv.n += 1
            rm_dv = dv.n

            @sch.on("scalar")
            def _(e, w=rm_dv, osl=osb_sl):
                e.wait_ge(dv.h, w)
                e.activation(rms[:, 4:8], rms[:, 0:4], AF.Ln,
                             scale=np.float32(1.0 / DV), bias=np.float32(EPS))
                e.activation(rms[:, 4:8], rms[:, 4:8], AF.Exp,
                             scale=np.float32(-0.5))
                for h in range(4):
                    e.mul(osl[:, h * 128:(h + 1) * 128],
                          osl[:, h * 128:(h + 1) * 128], rms[:, 4 + h:5 + h])
                e.fin(ac.h, 1)
            ac.n += 1
            zg_ac = ac.n

            @sch.on("gpsimd")
            def _(e, t=t, c=c, par=par, w=zg_ac, w2=sog.n, osl=osb_sl):
                e.wait_ge(ac.h, w)
                if w2:
                    e.wait_ge(sog.h, w2)
                e.tensor_tensor(osl, osl, nw_sb[:], ALU.mult)
                e.tensor_tensor(ogt[:], osl,
                                zsil4[:, c * 512:(c + 1) * 512], ALU.mult)
                e.fin(po.h, 1)
                e.dma_start(out=og[t * TILE + c * 128:t * TILE + (c + 1) * 128, :],
                            in_=ogt[:]).then_inc(sog.h, 16)
            po.n += 1
            sog.n += 16
        rel_pe[t + 2] = pe.n
        rel_po[t + 2] = po.n
        rel_dv[t + 2] = dv.n

    og_done = sog.n

    ccs = Ctr(sems, nc, "ccs")

    @sch.on("gpsimd")
    def _(e, w=og_done):
        e.wait_ge(sog.h, w)
        from concourse.aot_env import allow_ici_all_to_all
        with allow_ici_all_to_all():
            e.collective_compute(
                "AllToAll", ALU.bypass,
                replica_groups=[list(range(NCORE))],
                ins=[cc_in_h.ap().opt()],
                outs=[cc_out_h.ap().opt()],
            ).then_inc(ccs.h, 1)

    for eng_ in ENGINES:
        @sch.on(eng_)
        def _(e):
            e.wait_ge(ccs.h, 1)

    ctx.close()
    es2 = ExitStack()

    emit_phase2(nc, sch, sems, es2, x2, xT2, w1_p, w2_p, w3_p, wo_p,
                b1_in, b2_in, b3_in, outf, xn2_d, cc_out)

    chain_sems = {eng: sems.enter_context(nc.semaphore(f"ch_{eng}"))
                  for eng in ("vector", "scalar", "gpsimd")}
    with sems, es2, nc.Block() as block:
        for eng in ENGINES:
            steps = sch.streams[eng]

            def runner(e, steps=steps, ch=chain_sems.get(eng)):
                e = EngRec(e, chain=ch)
                for fn in steps:
                    fn(e)
            getattr(block, eng)(runner)
    return nc



def _l1_inputs(inputs):
    x = np.ascontiguousarray(
        np.asarray(inputs["L0_attn_hidden_states"], np.float32).reshape(S, D))
    W_qkv = np.asarray(inputs["L0_attn_W_qkv"], np.float32)
    W_z = np.asarray(inputs["L0_attn_W_z"], np.float32)
    W_a = np.asarray(inputs["L0_attn_W_a"], np.float32)
    W_b = np.asarray(inputs["L0_attn_W_b"], np.float32)
    conv_w = np.asarray(inputs["L0_attn_conv_w"], np.float32)
    A_log = np.asarray(inputs["L0_attn_A_log"], np.float32)
    dt_bias = np.asarray(inputs["L0_attn_dt_bias"], np.float32)
    norm_w = np.asarray(inputs["L0_attn_norm_w"], np.float32)

    mS = np.where(np.triu(np.ones((128, 128), np.float32), 1) > 0, 0., MASKVAL
                  ).astype(np.float32)
    mI = np.where(np.triu(np.ones((128, 128), np.float32), 0) > 0, 0., MASKVAL
                  ).astype(np.float32)
    I128 = np.eye(128, dtype=np.float32)
    U128 = np.triu(np.ones((128, 128), np.float32))
    L128 = np.tril(np.ones((128, 128), np.float32), -1)
    ones_r = np.ones((33, 128), np.float32)
    ones_c = np.ones((128, 1), np.float32)

    maps = []
    for c in range(NCORE):
        j0, j1 = 2 * c, 2 * c + 1
        hs = [4 * c + i for i in range(4)]
        rows = np.r_[np.arange(j0 * 128, (j0 + 1) * 128),
                     np.arange(j1 * 128, (j1 + 1) * 128),
                     np.arange(KEY + j0 * 128, KEY + (j0 + 1) * 128),
                     np.arange(KEY + j1 * 128, KEY + (j1 + 1) * 128)]
        vrows = np.concatenate([np.arange(2 * KEY + h * 128, 2 * KEY + (h + 1) * 128)
                                for h in hs])
        rows = np.r_[rows, vrows]
        Wsl = W_qkv[rows]
        wqkv_pp = _pretile(np.ascontiguousarray(Wsl.T), 16, 8)
        zrows = np.concatenate([np.arange(h * 128, (h + 1) * 128) for h in hs])
        Wz_sl = W_z[zrows]
        Wzab = np.concatenate([Wz_sl, W_a[hs], W_b[hs]], axis=0)
        ZT = np.ascontiguousarray(Wzab.T)
        wzab_pp = np.ascontiguousarray(
            ZT.reshape(16, 128, 520).transpose(1, 0, 2).reshape(128, 16 * 520))
        cw = conv_w[rows]
        conv_pp = np.ascontiguousarray(
            cw.reshape(8, 128, 4).transpose(1, 0, 2).reshape(128, 32))
        maps.append(dict(
            x_tok=x, wqkv_p=wqkv_pp, wzab_p=wzab_pp, conv_in=conv_pp,
            alog_r=np.ascontiguousarray(A_log[hs][None]),
            dtb_r=np.ascontiguousarray(dt_bias[hs][None]),
            nw_in=np.ascontiguousarray(
                np.broadcast_to(np.tile(norm_w, 4)[None], (128, 512))),
            mS_in=mS, mI_in=mI, I_in=I128, U_in=U128, L_in=L128,
            or_in=ones_r, oc_in=ones_c,
        ))
    return maps





TOK2 = S // NCORE
NKT2 = TOK2 // 512


def emit_phase2(nc, sch, sems, ctx, x2, xT2, w1_p, w2_p, w3_p, wo_p,
                b1_in, b2_in, b3_in, outf, xn2_d, cc_out):
    sb = lambda n_, sh, d_=F32: ctx.enter_context(nc.sbuf_tensor(n_, sh, d_))
    psa = lambda n_, sh, d_=F32: ctx.enter_context(nc.psum_tensor(n_, sh, d_))

    xt2 = sb("xt2", [128, D])
    xnt2 = sb("xnt2", [128, D], BF16)
    lnst2 = sb("lnst2", [128, 8])
    xnf2 = sb("xnf2", [128, 16 * TOK2], BF16)
    xTt = xnf2[:, 0:2 * TOK2].bitcast(F32)
    outt = xnf2[:, 2 * TOK2:4 * TOK2].bitcast(F32)
    h1f = sb("h1f", [128, 32 * TOK2], BF16)
    h2f = sb("h2f", [128, 32 * TOK2], BF16)
    oT = h1f
    wblk = [sb(f"wblk{i}", [128, 32 * 128], BF16) for i in range(2)]
    wblk2 = [sb(f"wblk2{i}", [128, 32 * 128], BF16) for i in range(2)]
    b1s = sb("b1s", [128, 32])
    b2s = sb("b2s", [128, 32])
    b3s = sb("b3s", [128, 16])


    PA = [psa(f"PA{i}", [128, 512]) for i in range(4)]

    sx2 = Ctr(sems, nc, "sx2")
    sxw2 = Ctr(sems, nc, "sxw2")
    sxr2 = Ctr(sems, nc, "sxr2")
    sor2 = Ctr(sems, nc, "sor2")
    swl = [Ctr(sems, nc, "swl0"), Ctr(sems, nc, "swl1")]
    sw2 = [Ctr(sems, nc, "sw2a"), Ctr(sems, nc, "sw2b")]
    sw3 = [Ctr(sems, nc, "sw3a"), Ctr(sems, nc, "sw3b")]
    sxt2 = Ctr(sems, nc, "sxt2")
    sot = Ctr(sems, nc, "sot")
    spre2 = Ctr(sems, nc, "spre2")
    pe = Ctr(sems, nc, "pe2")
    dv = Ctr(sems, nc, "dv2")
    ac = Ctr(sems, nc, "ac2")

    @sch.on("sync")
    def _(e):
        e.dma_start(out=b1s[:], in_=b1_in).then_inc(spre2.h, 16)
        e.dma_start(out=b2s[:], in_=b2_in).then_inc(spre2.h, 16)
        e.dma_start(out=b3s[:], in_=b3_in).then_inc(spre2.h, 16)
    spre2.n += 48
    PRE2 = spre2.n

    for i in range(TOK2 // 128):
        @sch.on("sync")
        def _(e, i=i, w=dv.n):
            if w:
                e.wait_ge(dv.h, w)
            e.dma_start(out=xt2[:], in_=x2[i * 128:(i + 1) * 128, :]
                        ).then_inc(sx2.h, 16)
        sx2.n += 16
        ld_n = sx2.n

        @sch.on("scalar")
        def _(e, w=ld_n, w2=sxw2.n):
            e.wait_ge(sx2.h, w)
            if w2:
                e.wait_ge(sxw2.h, w2)
            e.activation(xnt2[:], xt2[:], AF.Square,
                         accum_out=lnst2[:, 0:1]).then_inc(ac.h, 1)
        ac.n += 1
        sq_n = ac.n

        @sch.on("vector")
        def _(e, w=ld_n, w2=sq_n):
            e.wait_ge(sx2.h, w)
            e.tensor_reduce(lnst2[:, 1:2], xt2[:], AX.X, ALU.add)
            e.wait_ge(ac.h, w2)
            e.tensor_scalar_mul(lnst2[:, 2:3], lnst2[:, 1:2], 1.0 / D)
            e.tensor_tensor(lnst2[:, 3:4], lnst2[:, 2:3], lnst2[:, 2:3], ALU.mult)
            e.tensor_scalar(lnst2[:, 4:5], lnst2[:, 0:1], 1.0 / D, None, ALU.mult)
            e.tensor_tensor(lnst2[:, 4:5], lnst2[:, 4:5], lnst2[:, 3:4],
                            ALU.subtract)
            e.tensor_scalar(lnst2[:, 4:5], lnst2[:, 4:5], LN_EPS, None,
                            ALU.add).then_inc(dv.h, 1)
        dv.n += 1
        va_n = dv.n

        @sch.on("scalar")
        def _(e, w=va_n):
            e.wait_ge(dv.h, w)
            e.activation(lnst2[:, 5:6], lnst2[:, 4:5], AF.Ln)
            e.activation(lnst2[:, 6:7], lnst2[:, 5:6], AF.Exp,
                         scale=np.float32(-0.5)).then_inc(ac.h, 1)
        ac.n += 1
        rs_n = ac.n

        @sch.on("vector")
        def _(e, w=rs_n):
            e.wait_ge(ac.h, w)
            e.tensor_scalar(xnt2[:], xt2[:], lnst2[:, 2:3],
                            lnst2[:, 6:7], ALU.subtract, ALU.mult).then_inc(dv.h, 1)
        dv.n += 1
        nm_n = dv.n

        @sch.on("sync")
        def _(e, w=nm_n, i=i):
            e.wait_ge(dv.h, w)
            e.dma_start(out=xn2_d[i * 128:(i + 1) * 128, :],
                        in_=xnt2[:]).then_inc(sxw2.h, 16)
        sxw2.n += 16
    ln_done = (sxw2.n, dv.n)

    @sch.on("sync")
    def _(e, w=ln_done[0]):
        e.wait_ge(sxw2.h, w)
        for kc in range(16):
            e.dma_start(out=xnf2[:, kc * TOK2:(kc + 1) * TOK2],
                        in_=xn2_d[:, kc * 128:(kc + 1) * 128],
                        transpose=True).then_inc(sxr2.h, 16)
    sxr2.n += 256
    xnf_n = sxr2.n

    def ffn_layer(name, w_pret, nK, nM, src_sb, src_cols, dst_sb, bias_sb,
                  relu, wbufs, wsem):
        ldw = []
        ev_ns = []
        for m in range(nM):
            @sch.on("gpsimd")
            def _(e, m=m, wp=w_pret, wbufs=wbufs, nK=nK):
                if m >= 2:
                    e.wait_ge(pe.h, ldw[m - 2][1])
                e.dma_start(out=wbufs[m % 2][:, 0:nK * 128],
                            in_=wp[:, m * nK * 128:(m + 1) * nK * 128]
                            ).then_inc(wsem[m % 2].h, 16)
            wsem[m % 2].n += 16
            ldw.append([wsem[m % 2].n, None])

            @sch.on("tensor")
            def _(e, m=m, w=wsem[m % 2].n, wbufs=wbufs, nK=nK, src_sb=src_sb,
                  src_cols=src_cols, wev=(ev_ns[m - 2] if m >= 2 else 0)):
                e.wait_ge(wsem[m % 2].h, w)
                if m == 0 and src_sb is xnf2:
                    e.wait_ge(sxr2.h, xnf_n)
                if wev:
                    e.wait_ge(ac.h, wev)
                for ti in range(NKT2):
                    bank = PA[(m % 2) * NKT2 + ti]
                    for kc in range(nK):
                        e.matmul(bank[:],
                                 wbufs[m % 2][:, kc * 128:(kc + 1) * 128],
                                 src_sb[:, kc * src_cols + ti * 512:
                                        kc * src_cols + (ti + 1) * 512],
                                 start=(kc == 0), stop=(kc == nK - 1))
                e.fin(pe.h, 1)
            pe.n += 1
            ldw[m][1] = pe.n
            mm_n = pe.n

            @sch.on("scalar")
            def _(e, m=m, w=mm_n, dst_sb=dst_sb, bias_sb=bias_sb, relu=relu,
                  w2=PRE2):
                e.wait_ge(pe.h, w)
                e.wait_ge(spre2.h, w2)
                for ti in range(NKT2):
                    e.activation(dst_sb[:, m * TOK2 + ti * 512:
                                        m * TOK2 + (ti + 1) * 512],
                                 PA[(m % 2) * NKT2 + ti][:],
                                 AF.Relu if relu else AF.Copy,
                                 bias=bias_sb[:, m:m + 1]).then_inc(ac.h, 1)
            ac.n += NKT2
            ev_ns.append(ac.n)
        return ac.n

    h1_ac = ffn_layer("w1", w1_p, 16, 32, xnf2, TOK2, h1f, b1s, True,
                      wblk, swl)

    @sch.on("tensor")
    def _(e, w=h1_ac):
        e.wait_ge(ac.h, w)
        e.fin(pe.h, 1)
    pe.n += 1

    h2_ac = ffn_layer("w2", w2_p, 32, 32, h1f, TOK2, h2f, b2s, True,
                      wblk2, sw2)
    w2_mm_pe = pe.n

    @sch.on("sync")
    def _(e, w=w2_mm_pe):
        e.wait_ge(pe.h, w)
        for kc in range(32):
            j, m_ = kc // 4, kc % 4
            e.dma_start(out=oT[:, kc * TOK2:(kc + 1) * TOK2],
                        in_=cc_out[j * TOK2:(j + 1) * TOK2,
                                   m_ * 128:(m_ + 1) * 128],
                        transpose=True).then_inc(sor2.h, 16)
    sor2.n += 512
    oT_n = sor2.n

    @sch.on("tensor")
    def _(e, w=h2_ac, w2=oT_n):
        e.wait_ge(ac.h, w)
        e.wait_ge(sor2.h, w2)
        e.fin(pe.h, 1)
    pe.n += 1

    cb_ns = []
    for m in range(16):
        @sch.on("gpsimd")
        def _(e, m=m, w=(pe.n if m >= 2 else 0), w2=w2_mm_pe, w3=dv.n):
            if m == 0:
                e.wait_ge(pe.h, w2)
            if w:
                e.wait_ge(pe.h, w - 1)
            if w3 and m > 0:
                e.wait_ge(dv.h, w3)
            e.dma_start(out=wblk[m % 2][:], in_=w3_p[:, m * 4096:(m + 1) * 4096]
                        ).then_inc(sw3[m % 2].h, 16)
            e.dma_start(out=wblk2[m % 2][:], in_=wo_p[:, m * 4096:(m + 1) * 4096]
                        ).then_inc(sw3[m % 2].h, 16)
            e.dma_start(out=xTt, in_=xT2[m * 128:(m + 1) * 128, :]
                        ).then_inc(sxt2.h, 16)
        sw3[m % 2].n += 32
        sxt2.n += 16
        wl_n = sw3[m % 2].n
        xt_n = sxt2.n

        @sch.on("tensor")
        def _(e, m=m, w=wl_n, mp=m % 2, wev=(cb_ns[m - 2] if m >= 2 else 0)):
            e.wait_ge(sw3[mp].h, w)
            if wev:
                e.wait_ge(dv.h, wev)
            for ti in range(NKT2):
                bank = PA[(m % 2) * NKT2 + ti]
                for kc in range(32):
                    e.matmul(bank[:], wblk[m % 2][:, kc * 128:(kc + 1) * 128],
                             h2f[:, kc * TOK2 + ti * 512:kc * TOK2 + (ti + 1) * 512],
                             start=(kc == 0), stop=False,
                             skip_group_check=True)
                for kc in range(32):
                    e.matmul(bank[:], wblk2[m % 2][:, kc * 128:(kc + 1) * 128],
                             oT[:, kc * TOK2 + ti * 512:kc * TOK2 + (ti + 1) * 512],
                             start=False, stop=(kc == 31),
                             skip_group_check=True)
            e.fin(pe.h, 1)
        pe.n += 1
        mm_n = pe.n

        @sch.on("vector")
        def _(e, m=m, w=mm_n, w2=xt_n, w3=sot.n):
            e.wait_ge(pe.h, w)
            e.wait_ge(sxt2.h, w2)
            if w3:
                e.wait_ge(sot.h, w3)
            for ti in range(NKT2):
                e.scalar_tensor_tensor(outt[:, ti * 512:(ti + 1) * 512],
                                       PA[(m % 2) * NKT2 + ti][:], b3s[:, m:m + 1],
                                       xTt[:, ti * 512:(ti + 1) * 512],
                                       ALU.add, ALU.add)
            e.fin(dv.h, 1)
        dv.n += 1
        cb_n = dv.n
        cb_ns.append(cb_n)

        @sch.on("sync")
        def _(e, m=m, w=cb_n):
            e.wait_ge(dv.h, w)
            e.dma_start(out=outf[m * 128:(m + 1) * 128, :],
                        in_=outt).then_inc(sot.h, 16)
        sot.n += 16

    done = sot.n

    @sch.on("gpsimd")
    def _(e, w=done):
        e.wait_ge(sot.h, w)


def _l2_inputs(inputs):
    x = np.asarray(inputs["L0_attn_hidden_states"], np.float32).reshape(S, D)
    xT = np.ascontiguousarray(x.T)
    W1 = np.asarray(inputs["L0_ffn_W1"], np.float32)
    W2 = np.asarray(inputs["L0_ffn_W2"], np.float32)
    W3 = np.asarray(inputs["L0_ffn_W3"], np.float32)
    Wo = np.asarray(inputs["L0_attn_W_out"], np.float32)
    b1 = np.asarray(inputs["L0_ffn_b1"], np.float32)
    b2 = np.asarray(inputs["L0_ffn_b2"], np.float32)
    b3 = np.asarray(inputs["L0_ffn_b3"], np.float32)

    w1_p = _pretile(np.ascontiguousarray(W1.T), 16, 32)
    w2_p = _pretile(np.ascontiguousarray(W2.T), 32, 32)
    w3_p = _pretile(np.ascontiguousarray(W3.T), 32, 16)
    wo_p = _pretile(np.ascontiguousarray(Wo.T), 32, 16)
    b1_t = np.ascontiguousarray(b1.reshape(32, 128).T)
    b2_t = np.ascontiguousarray(b2.reshape(32, 128).T)
    b3_t = np.ascontiguousarray(b3.reshape(16, 128).T)

    maps = []
    for c in range(NCORE):
        t0 = c * TOK2
        maps.append(dict(
            x2=np.ascontiguousarray(x[t0:t0 + TOK2]),
            xT2=np.ascontiguousarray(xT[:, t0:t0 + TOK2]),
            w1_p=w1_p, w2_p=w2_p, w3_p=w3_p, wo_p=wo_p,
            b1_in=b1_t, b2_in=b2_t, b3_in=b3_t,
        ))
    return maps


def kernel(**inputs):
    nc = build_fused()
    m1 = _l1_inputs(inputs)
    m2 = _l2_inputs(inputs)
    maps = [dict(a, **b) for a, b in zip(m1, m2)]
    res = run_bass_kernel_spmd(nc, maps, core_ids=list(range(NCORE)))
    outs = [np.asarray(res.results[c]["outf"]) for c in range(NCORE)]
    out = np.concatenate([o.T for o in outs], axis=0)
    return out.reshape(1, S, D)


if __name__ == "__main__":
    import time, tempfile
    t0 = time.time()
    nc = build_fused()
    t1 = time.time()
    js = nc.to_json_bytes()
    from concourse.bass_utils import compile_bir_kernel
    d = tempfile.mkdtemp()
    f = compile_bir_kernel(js, d)
    t2 = time.time()
    import os
    print(f"fused build={t1-t0:.1f}s json={len(js)/1e6:.1f}MB "
          f"compile={t2-t1:.1f}s neff={os.path.getsize(f)/1e6:.1f}MB")
